# version 01 (final)
# baseline (speedup 1.0000x reference)
import contextlib
import numpy as np
import concourse.bass as bass
import concourse.mybir as mybir
from concourse.bass_utils import run_bass_kernel_spmd

F32 = mybir.dt.float32
BF16 = mybir.dt.bfloat16
AF = mybir.ActivationFunctionType
ALU = mybir.AluOpType
AX = mybir.AxisListType

B, T, C = 2, 8192, 1024
NCORE = 8
TL = 2048
PAD = 128
CH = C // 128
TT = 512
NT = TL // TT
LN_EPS = 1e-5
RG = [[0, 1, 2, 3, 4, 5, 6, 7]]


class _NullBlock:
    def __enter__(self):
        return self

    def __exit__(self, *a):
        return False

    def _skip(self, f):
        pass
    sync = vector = scalar = tensor = gpsimd = _skip


class Sem:
    def __init__(self, h):
        self.h = h
        self.n = 0

    def inc(self, instr, k=1):
        instr.then_inc(self.h, k)
        self.n += k
        return self.n

    def wait(self, eng, v=None):
        eng.wait_ge(self.h, self.n if v is None else v)


class Ctx:
    def __init__(self, nc, stack):
        self.nc = nc
        self.stack = stack
        self.d = {}
        self.sems = {}

    def S(self, name):
        if name not in self.sems:
            self.sems[name] = Sem(self.stack.enter_context(self.nc.semaphore(name)))
        return self.sems[name]

    def dram(self, name, shape, dtype, kind="Internal"):
        if name not in self.d:
            self.d[name] = self.nc.dram_tensor(name, shape, dtype, kind=kind).ap()
        return self.d[name]

    def sb(self, stack, name, shape, dtype):
        return stack.enter_context(self.nc.sbuf_tensor(name, shape, dtype))

    def ps(self, stack, name, shape, dtype):
        return stack.enter_context(self.nc.psum_tensor(name, shape, dtype))


def fmv(d):
    return d.rearrange("(ch p) t -> p ch t", p=128)


def colv(d):
    return d[:, :]


def dinc(eng, sem, k=1):
    sem.inc(eng.drain(), k)


def emit_identity(g, ident, sem):
    g.memset(ident[:], 0.0)
    g.affine_select(out=ident[:], in_=ident[:], compare_op=ALU.not_equal,
                    fill=1.0, base=0, pattern=[[-1, ident.shape[1]]],
                    channel_multiplier=1)
    dinc(g, sem)


def phase_prep_masks(ctx):
    nc = ctx.nc
    sPO = ctx.S("po")
    m = {}
    m["id16"] = ctx.sb(ctx.stack, "mk_id16", [128, 128], BF16)
    m["id32"] = ctx.sb(ctx.stack, "mk_id32", [128, 128], F32)
    m["cmask"] = ctx.sb(ctx.stack, "mk_cmask", [128, 4 * 512], F32)
    m["t1mask"] = ctx.sb(ctx.stack, "mk_t1m", [128, 4 * 512], BF16)
    m["lhmask"] = ctx.sb(ctx.stack, "mk_lhm", [128, 4 * 512], BF16)
    ctx.masks = m
    with nc.Block("prep_masks") as blk:
        def g0(g):
            for ident in (m["id16"], m["id32"]):
                g.memset(ident[:], 0.0)
                g.affine_select(out=ident[:], in_=ident[:],
                                compare_op=ALU.not_equal, fill=1.0, base=0,
                                pattern=[[-1, 128]], channel_multiplier=1)
            g.memset(m["cmask"][:], 0.0)
            g.memset(m["t1mask"][:], 1.0)
            g.memset(m["lhmask"][:], 1.0)
            for mi in range(4):
                sl = slice(mi * 512, (mi + 1) * 512)
                g.affine_select(out=m["cmask"][:, sl], in_=m["cmask"][:, sl],
                                compare_op=ALU.is_ge, fill=-1.0e30,
                                base=mi * 128, pattern=[[-1, 512]],
                                channel_multiplier=1)
                g.affine_select(out=m["t1mask"][:, sl], in_=m["t1mask"][:, sl],
                                compare_op=ALU.is_ge, fill=0.0,
                                base=-mi * 128, pattern=[[1, 512]],
                                channel_multiplier=-1)
                g.affine_select(out=m["lhmask"][:, sl], in_=m["lhmask"][:, sl],
                                compare_op=ALU.is_gt, fill=0.0,
                                base=mi * 128, pattern=[[-1, 512]],
                                channel_multiplier=1)
            dinc(g, sPO)
        blk.gpsimd(g0)


def phase_prep_w0(ctx):
    nc = ctx.nc
    names = [("Wk", "h_wkT"), ("Wv", "h_wvT"), ("Wr", "h_wrT"), ("Wo", "h_woT")]
    sLD, sPE, sDV, sPO, sST = (ctx.S("ld"), ctx.S("pe"), ctx.S("dv"),
                               ctx.S("po"), ctx.S("st"))
    with contextlib.ExitStack() as st:
        wrow = ctx.sb(st, "pw_row", [128, C], F32)
        wrow16 = ctx.sb(st, "pw_row16", [128, C], BF16)
        ident16 = ctx.masks["id16"]
        evac = ctx.sb(st, "pw_evac", [128, C], BF16)
        pt = [ctx.ps(st, f"pw_pt{i}", [128, 512], BF16) for i in range(2)]
        with nc.Block("prep_w0") as blk:
            for wn, hn in names:
                w, out = ctx.d[wn], ctx.d[hn]
                for co in range(CH):
                    def ld(s, co=co, w=w):
                        sPO.wait(s)
                        sLD.inc(s.dma_start(
                            out=wrow[:], in_=w[co * 128:(co + 1) * 128, :]), 16)
                    blk.sync(ld)

                    def cast(g):
                        sLD.wait(g)
                        sST.wait(g)
                        g.tensor_copy(out=wrow16[:], in_=wrow[:])
                        dinc(g, sPO)
                    blk.gpsimd(cast)

                    dv_base = sDV.n
                    pe_base = sPE.n

                    def pe(t, dv_base=dv_base):
                        sPO.wait(t)
                        sST.wait(t)
                        for ci in range(CH):
                            if ci >= 2:
                                t.wait_ge(sDV.h, dv_base + ci - 1)
                            t.transpose(pt[ci % 2][:, 0:128],
                                        wrow16[:, ci * 128:(ci + 1) * 128], ident16[:])
                            t.drain().then_inc(sPE.h, 1)
                            sPE.n += 1
                    blk.tensor(pe)

                    def dv(v, pe_base=pe_base):
                        for ci in range(CH):
                            v.wait_ge(sPE.h, pe_base + ci + 1)
                            sDV.inc(v.tensor_copy(
                                out=evac[:, ci * 128:(ci + 1) * 128],
                                in_=pt[ci % 2][:, 0:128]), 1)
                    blk.vector(dv)

                    def stq(s, co=co, out=out):
                        sDV.wait(s)
                        sST.inc(s.dma_start(
                            out=out.rearrange("(ci p) c -> p ci c", p=128)[:, :, co * 128:(co + 1) * 128],
                            in_=evac[:].rearrange("p (ci c) -> p ci c", ci=CH)), 16)
                    blk.sync(stq)
            blk.sync(lambda s: sST.wait(s))


def phase_ln0(ctx):
    nc = ctx.nc
    xin = ctx.d["xin"]
    h_x, h_xn16, h_xn2f = ctx.d["h_x"], ctx.d["h_xn16"], ctx.d["h_xn2f"]
    sLD, sPE, sACT, sDV, sPO, sST = (ctx.S("ld"), ctx.S("pe"), ctx.S("act"),
                                     ctx.S("dv"), ctx.S("po"), ctx.S("st"))
    with contextlib.ExitStack() as st:
        xtm = ctx.sb(st, "ln_xtm", [128, 4 * C], F32)
        xfm = ctx.sb(st, "ln_xfm", [128, CH * TT], F32)
        xnt = ctx.sb(st, "ln_xnt", [128, CH * TT], F32)
        xn2t = ctx.sb(st, "ln_xn2t", [128, CH * TT], F32)
        xn16t = ctx.sb(st, "ln_xn16t", [128, CH * TT], BF16)
        sq = ctx.sb(st, "ln_sq", [128, CH * TT], F32)
        ident32 = ctx.masks["id32"]
        ones_col = ctx.sb(st, "ln_ones_col", [128, 1], F32)
        ones_row = ctx.sb(st, "ln_ones_row", [1, 128], F32)
        rows = ctx.sb(st, "ln_rows", [1, 6 * TT], F32)
        gcol = ctx.sb(st, "ln_gcol", [128, CH], F32)
        bcol = ctx.sb(st, "ln_bcol", [128, CH], F32)
        psc = ctx.sb(st, "ln_psc", [128, 1], F32)
        epsc = ctx.sb(st, "ln_eps", [1, 1], F32)
        ptr = [ctx.ps(st, f"ln_pt{i}", [128, 512], F32) for i in range(2)]
        prs = ctx.ps(st, "ln_prs", [1, 512], F32)
        prq = ctx.ps(st, "ln_prq", [1, 512], F32)
        pbm = ctx.ps(st, "ln_pbm", [128, 512], F32)
        pbr = ctx.ps(st, "ln_pbr", [128, 512], F32)
        M0, R0, M2, R2, SC1, SC2 = 0, TT, 2 * TT, 3 * TT, 4 * TT, 5 * TT
        tiles = [(0, PAD)] + [(PAD + i * TT, TT) for i in range(NT)]

        with nc.Block("ln0") as blk:
            def init_v(v):
                v.memset(ones_col[:], 1.0)
                v.memset(epsc[:], float(LN_EPS))
                sDV.inc(v.memset(ones_row[:], 1.0), 1)
            blk.vector(init_v)

            def init_ld(s):
                sLD.inc(s.dma_start(out=gcol[:], in_=colv(ctx.d["ln0_g"])), 16)
                sLD.inc(s.dma_start(out=bcol[:], in_=colv(ctx.d["ln0_b"])), 16)
                sLD.inc(s.dma_start(out=psc[:], in_=ctx.d["prev_scale"][:, :]), 16)
            blk.sync(init_ld)

            for ti, (c0, w) in enumerate(tiles):
                nsub = w // 128

                def ld(s, c0=c0, w=w, ti=ti, nsub=nsub):
                    if ti >= 1:
                        sPE.wait(s)
                    sLD.inc(s.dma_start(
                        out=xtm[:, 0:nsub * C].rearrange("p (n c) -> p n c", n=nsub),
                        in_=xin.rearrange("(n p) c -> p n c", p=128)[:, c0 // 128:(c0 + w) // 128]), 16)
                blk.sync(ld)

                dv_base = sDV.n
                pe_base = sPE.n

                def pe_t(t, w=w, nsub=nsub, dv_base=dv_base):
                    sLD.wait(t)
                    sPO.wait(t)
                    sDV.wait(t)
                    for ch in range(CH):
                        if ch >= 2:
                            t.wait_ge(sDV.h, dv_base + ch - 1)
                        for si in range(nsub):
                            t.transpose(ptr[ch % 2][:, si * 128:(si + 1) * 128],
                                        xtm[:, si * C + ch * 128: si * C + ch * 128 + 128],
                                        ident32[:])
                        t.drain().then_inc(sPE.h, 1)
                        sPE.n += 1
                blk.tensor(pe_t)

                def dv_t(v, w=w, pe_base=pe_base):
                    for ch in range(CH):
                        v.wait_ge(sPE.h, pe_base + ch + 1)
                        sDV.inc(v.tensor_copy(out=xfm[:, ch * TT: ch * TT + w],
                                              in_=ptr[ch % 2][:, 0:w]), 1)
                blk.vector(dv_t)

                def stx(s, c0=c0, w=w):
                    sDV.wait(s)
                    sST.inc(s.dma_start(
                        out=fmv(h_x)[:, :, c0:c0 + w],
                        in_=xfm[:].rearrange("p (ch t) -> p ch t", ch=CH)[:, :, 0:w]), 16)
                blk.sync(stx)

                for rnd, (src, dst, Mo, Ro) in enumerate(
                        ((xfm, xnt, M0, R0), (xnt, xn2t, M2, R2))):
                    def pe_s(t, w=w, src=src):
                        sDV.wait(t)
                        for ch in range(CH):
                            t.matmul(prs[:, 0:w], ones_col[:],
                                     src[:, ch * TT:ch * TT + w],
                                     start=(ch == 0), stop=(ch == CH - 1))
                        t.drain().then_inc(sPE.h, 1)
                        sPE.n += 1
                    blk.tensor(pe_s)

                    def act_sq(a, w=w, src=src):
                        sDV.wait(a)
                        last = None
                        for ch in range(CH):
                            last = a.activation(out=sq[:, ch * TT:ch * TT + w],
                                                in_=src[:, ch * TT:ch * TT + w],
                                                func=AF.Square)
                        _ = last
                        dinc(a, sACT)
                    blk.scalar(act_sq)

                    def pe_sq(t, w=w):
                        sACT.wait(t)
                        for ch in range(CH):
                            t.matmul(prq[:, 0:w], ones_col[:],
                                     sq[:, ch * TT:ch * TT + w],
                                     start=(ch == 0), stop=(ch == CH - 1))
                        t.drain().then_inc(sPE.h, 1)
                        sPE.n += 1
                    blk.tensor(pe_sq)

                    def dv_rows(v, w=w, Mo=Mo):
                        sPE.wait(v)
                        v.tensor_scalar_mul(rows[:, Mo:Mo + w], prs[:, 0:w], 1.0 / C)
                        v.tensor_scalar_mul(rows[:, SC1:SC1 + w], prq[:, 0:w], 1.0 / C)
                        v.drain()
                        v.tensor_mul(rows[:, SC2:SC2 + w], rows[:, Mo:Mo + w],
                                     rows[:, Mo:Mo + w])
                        v.drain()
                        v.tensor_sub(rows[:, SC1:SC1 + w], rows[:, SC1:SC1 + w],
                                     rows[:, SC2:SC2 + w])
                        dinc(v, sDV)
                    blk.vector(dv_rows)

                    def act_rstd(a, w=w, Ro=Ro):
                        sDV.wait(a)
                        a.activation(out=rows[:, SC2:SC2 + w], in_=rows[:, SC1:SC1 + w],
                                     func=AF.Ln, bias=epsc[0:1, 0:1])
                        a.drain()
                        a.activation(out=rows[:, Ro:Ro + w],
                                     in_=rows[:, SC2:SC2 + w],
                                     func=AF.Exp, scale=-0.5)
                        dinc(a, sACT)
                    blk.scalar(act_rstd)

                    def pe_b(t, w=w, Mo=Mo, Ro=Ro):
                        sACT.wait(t)
                        sDV.wait(t)
                        t.matmul(pbm[:, 0:w], ones_row[:], rows[0:1, Mo:Mo + w],
                                 start=True, stop=True)
                        t.matmul(pbr[:, 0:w], ones_row[:], rows[0:1, Ro:Ro + w],
                                 start=True, stop=True)
                        t.drain().then_inc(sPE.h, 1)
                        sPE.n += 1
                    blk.tensor(pe_b)

                    def dv_app(v, w=w, src=src, dst=dst, rnd=rnd, ti=ti):
                        sPE.wait(v)
                        last = None
                        for ch in range(CH):
                            v.scalar_tensor_tensor(
                                out=dst[:, ch * TT:ch * TT + w], in0=pbm[:, 0:w],
                                scalar=-1.0, in1=src[:, ch * TT:ch * TT + w],
                                op0=ALU.mult, op1=ALU.add)
                            last = v.tensor_mul(dst[:, ch * TT:ch * TT + w],
                                                dst[:, ch * TT:ch * TT + w], pbr[:, 0:w])
                            if rnd == 1:
                                last = v.tensor_scalar(
                                    out=dst[:, ch * TT:ch * TT + w],
                                    in0=dst[:, ch * TT:ch * TT + w],
                                    scalar1=gcol[:, ch:ch + 1], scalar2=bcol[:, ch:ch + 1],
                                    op0=ALU.mult, op1=ALU.add)
                        if rnd == 1 and ti == 0:
                            sLD.wait(v)
                            for ch in range(CH):
                                last = v.tensor_scalar_mul(
                                    dst[:, ch * TT + PAD - 1:ch * TT + PAD],
                                    dst[:, ch * TT + PAD - 1:ch * TT + PAD],
                                    psc[:, 0:1])
                        _ = last
                        dinc(v, sDV)
                    blk.vector(dv_app)

                if c0 >= PAD:
                    def po_c(g, w=w):
                        sDV.wait(g)
                        sST.wait(g)
                        last = None
                        for ch in range(CH):
                            last = g.tensor_copy(out=xn16t[:, ch * TT:ch * TT + w],
                                                 in_=xnt[:, ch * TT:ch * TT + w])
                        _ = last
                        dinc(g, sPO)
                    blk.gpsimd(po_c)

                def sto(s, c0=c0, w=w, ti=ti):
                    sDV.wait(s)
                    sST.inc(s.dma_start(
                        out=ctx.d["h_rows"][2 * ti:2 * ti + 1, :],
                        in_=rows[0:1, :]), 16)
                    sST.inc(s.dma_start(
                        out=fmv(ctx.d["h_xnf"])[:, :, c0:c0 + w],
                        in_=xnt[:].rearrange("p (ch t) -> p ch t", ch=CH)[:, :, 0:w]), 16)
                    sST.inc(s.dma_start(
                        out=fmv(h_xn2f)[:, :, c0:c0 + w],
                        in_=xn2t[:].rearrange("p (ch t) -> p ch t", ch=CH)[:, :, 0:w]), 16)
                    if c0 >= PAD:
                        sPO.wait(s)
                        sST.inc(s.dma_start(
                            out=fmv(h_xn16)[:, :, c0 - PAD:c0 - PAD + w],
                            in_=xn16t[:].rearrange("p (ch t) -> p ch t", ch=CH)[:, :, 0:w]), 16)
                blk.sync(sto)
            blk.sync(lambda s: sST.wait(s))


def phase_rwkv_a(ctx):
    nc = ctx.nc
    h_xn2f, h_num, h_den, h_sigr = (ctx.d["h_xn2f"], ctx.d["h_num"],
                                    ctx.d["h_den"], ctx.d["h_sigr"])
    cc_fin = ctx.d["cc_fin"]
    sLD, sPE, sACT, sDV, sPO, sST = (ctx.S("ld"), ctx.S("pe"), ctx.S("act"),
                                     ctx.S("dv"), ctx.S("po"), ctx.S("st"))
    with contextlib.ExitStack() as st:
        mixk = ctx.sb(st, "ra_mixk", [128, CH], F32)
        mixv = ctx.sb(st, "ra_mixv", [128, CH], F32)
        mixr = ctx.sb(st, "ra_mixr", [128, CH], F32)
        tdraw = ctx.sb(st, "ra_tdraw", [128, CH], F32)
        tdc = ctx.sb(st, "ra_tdc", [128, CH], F32)
        fbuf = ctx.sb(st, "ra_fbuf", [128, 16], F32)
        ident32 = ctx.masks["id32"]
        ftp = ctx.sb(st, "ra_ftp", [16, 128], F32)
        stP = st.enter_context(contextlib.ExitStack())
        wkT = ctx.sb(stP, "ra_wkT", [128, CH * C], BF16)
        wvT = ctx.sb(stP, "ra_wvT", [128, CH * C], BF16)
        wrT = ctx.sb(stP, "ra_wrT", [128, CH * C], BF16)
        xn2b = ctx.sb(stP, "ra_xn2b", [128, CH * (TT + 1)], F32)
        uk = ctx.sb(stP, "ra_uk", [128, CH * TT], BF16)
        uv = ctx.sb(stP, "ra_uv", [128, CH * TT], BF16)
        ur = ctx.sb(stP, "ra_ur", [128, CH * TT], BF16)
        tmp = ctx.sb(stP, "ra_tmp", [128, 2 * TT], F32)
        ek = ctx.sb(stP, "ra_ek", [128, CH * TT], F32)
        kv = ctx.sb(stP, "ra_kv", [128, CH * TT], F32)
        srt = ctx.sb(stP, "ra_srt", [128, CH * TT], BF16)
        pk = [ctx.ps(st, f"ra_pk{i}", [128, TT], F32) for i in range(2)]
        pv = [ctx.ps(st, f"ra_pv{i}", [128, TT], F32) for i in range(2)]
        pr = [ctx.ps(st, f"ra_pr{i}", [128, TT], F32) for i in range(2)]
        ptf = ctx.ps(st, "ra_ptf", [16, 128], F32)

        with nc.Block("rwkv_a") as blk:
            def init(s):
                for buf, hn in ((wkT, "h_wkT"), (wvT, "h_wvT"), (wrT, "h_wrT")):
                    sLD.inc(s.dma_start(
                        out=buf[:].rearrange("p (ci c) -> p ci c", ci=CH),
                        in_=ctx.d[hn].rearrange("(ci p) c -> p ci c", p=128)), 16)
                for nm, dstb in (("tmk", mixk), ("tmv", mixv), ("tmr", mixr)):
                    sLD.inc(s.dma_start(out=dstb[:], in_=colv(ctx.d[nm])), 16)
                sLD.inc(s.dma_start(out=tdraw[:], in_=colv(ctx.d["time_decay"])), 16)
            blk.sync(init)

            def init_a(a):
                sLD.wait(a)
                a.activation(out=tdc[:], in_=tdraw[:], func=AF.Exp)
                dinc(a, sACT)
            blk.scalar(init_a)

            for tt in range(NT):
                t0 = PAD + tt * TT

                def ld(s, t0=t0, tt=tt):
                    if tt >= 1:
                        sDV.wait(s)
                    sLD.inc(s.dma_start(
                        out=xn2b[:].rearrange("p (ch t) -> p ch t", ch=CH),
                        in_=fmv(h_xn2f)[:, :, t0 - 1:t0 + TT]), 16)
                blk.sync(ld)

                def dv_mix(v, tt=tt):
                    sLD.wait(v)
                    sPE.wait(v)
                    for (mx, u) in ((mixk, uk), (mixv, uv), (mixr, ur)):
                        for ch in range(CH):
                            cur = xn2b[:, ch * (TT + 1) + 1: ch * (TT + 1) + 1 + TT]
                            prv = xn2b[:, ch * (TT + 1): ch * (TT + 1) + TT]
                            v.tensor_scalar_mul(tmp[:, 0:TT], cur, mx[:, ch:ch + 1])
                            v.scalar_tensor_tensor(
                                out=tmp[:, TT:2 * TT], in0=prv, scalar=mx[:, ch:ch + 1],
                                in1=prv, op0=ALU.mult, op1=ALU.subtract)
                            v.tensor_sub(u[:, ch * TT:(ch + 1) * TT],
                                         tmp[:, 0:TT], tmp[:, TT:2 * TT])
                    dinc(v, sDV)
                blk.vector(dv_mix)

                for co in range(CH):
                    def pe_p(t, co=co, tt=tt):
                        sDV.wait(t)
                        sACT.wait(t)
                        if co > 0 or tt > 0:
                            t.wait_ge(sDV.h, sDV.n)
                        for (wT, u, bank) in ((wkT, uk, pk[co % 2]),
                                              (wvT, uv, pv[co % 2]),
                                              (wrT, ur, pr[co % 2])):
                            for ci in range(CH):
                                t.matmul(bank[:],
                                         wT[:, ci * C + co * 128: ci * C + co * 128 + 128],
                                         u[:, ci * TT:(ci + 1) * TT],
                                         start=(ci == 0), stop=(ci == CH - 1))
                        t.drain().then_inc(sPE.h, 1)
                        sPE.n += 1
                    blk.tensor(pe_p)

                    def act_e(a, co=co, tt=tt):
                        sPE.wait(a)
                        if tt > 0 and co == 0:
                            sST.wait(a)
                        a.activation(out=ek[:, co * TT:(co + 1) * TT],
                                     in_=pk[co % 2][:], func=AF.Exp)
                        a.activation(out=srt[:, co * TT:(co + 1) * TT],
                                     in_=pr[co % 2][:], func=AF.Sigmoid)
                        dinc(a, sACT)
                    blk.scalar(act_e)

                    def dv_kv(v, co=co, tt=tt):
                        sACT.wait(v)
                        sDV.inc(v.tensor_mul(
                            kv[:, co * TT:(co + 1) * TT],
                            ek[:, co * TT:(co + 1) * TT],
                            pv[co % 2][:]), 1)
                    blk.vector(dv_kv)

                def st_r(s, tt=tt):
                    sACT.wait(s)
                    sDV.wait(s)
                    sST.inc(s.dma_start(
                        out=fmv(h_sigr)[:, :, tt * TT:(tt + 1) * TT],
                        in_=srt[:].rearrange("p (ch t) -> p ch t", ch=CH)), 16)
                    sST.inc(s.dma_start(
                        out=fmv(ctx.d["h_ek"])[:, :, tt * TT:(tt + 1) * TT],
                        in_=ek[:].rearrange("p (ch t) -> p ch t", ch=CH)), 16)
                    sST.inc(s.dma_start(
                        out=fmv(ctx.d["h_kv"])[:, :, tt * TT:(tt + 1) * TT],
                        in_=kv[:].rearrange("p (ch t) -> p ch t", ch=CH)), 16)
                blk.sync(st_r)

            stP.close()
            scanb = [ctx.sb(st, f"ra_scan{i}", [128, 2 * TL], F32) for i in range(2)]
            sinb = [ctx.sb(st, f"ra_sin{i}", [128, 2 * TL], F32) for i in range(2)]
            for ch in range(CH):
                buf = scanb[ch % 2]
                sin = sinb[ch % 2]

                def ld_scan(s, ch=ch, sin=sin):
                    if ch < 2:
                        sST.wait(s)
                    else:
                        sDV.wait(s)
                    sLD.inc(s.dma_start(out=sin[:, 0:TL],
                                        in_=ctx.d["h_kv"][ch * 128:(ch + 1) * 128, :]), 16)
                    sLD.inc(s.dma_start(out=sin[:, TL:2 * TL],
                                        in_=ctx.d["h_ek"][ch * 128:(ch + 1) * 128, :]), 16)
                blk.sync(ld_scan)

                def dv_scan(v, ch=ch, buf=buf, sin=sin):
                    sLD.wait(v)
                    if ch >= 2:
                        sST.wait(v)
                    v.tensor_tensor_scan(
                        out=buf[:, 0:TL],
                        data0=tdc[:, ch:ch + 1].broadcast_to([128, TL]),
                        data1=sin[:, 0:TL],
                        initial=0.0, op0=ALU.mult, op1=ALU.add)
                    v.tensor_tensor_scan(
                        out=buf[:, TL:2 * TL],
                        data0=tdc[:, ch:ch + 1].broadcast_to([128, TL]),
                        data1=sin[:, TL:2 * TL],
                        initial=0.0, op0=ALU.mult, op1=ALU.add)
                    v.drain()
                    v.tensor_copy(out=fbuf[:, ch:ch + 1], in_=buf[:, TL - 1:TL])
                    sDV.inc(v.tensor_copy(out=fbuf[:, 8 + ch:9 + ch],
                                          in_=buf[:, 2 * TL - 1:2 * TL]), 1)
                blk.vector(dv_scan)

                def st_scan(s, ch=ch, buf=buf):
                    sDV.wait(s)
                    sST.inc(s.dma_start(out=h_num[ch * 128:(ch + 1) * 128, :],
                                        in_=buf[:, 0:TL]), 16)
                    sST.inc(s.dma_start(out=h_den[ch * 128:(ch + 1) * 128, :],
                                        in_=buf[:, TL:2 * TL]), 16)
                blk.sync(st_scan)

            def pe_f(t):
                sDV.wait(t)
                sPO.wait(t)
                t.transpose(ptf[:], fbuf[:], ident32[:])
                t.drain().then_inc(sPE.h, 1)
                sPE.n += 1
            blk.tensor(pe_f)

            def dv_f(v):
                sPE.wait(v)
                sDV.inc(v.tensor_copy(out=ftp[:], in_=ptf[:]), 1)
            blk.vector(dv_f)

            def st_f(s):
                sDV.wait(s)
                sST.inc(s.dma_start(out=cc_fin[:, :], in_=ftp[:]), 16)
                sST.wait(s)
            blk.sync(st_f)


def phase_rwkv_s(ctx):
    nc = ctx.nc
    cc_fin, cc_fout = ctx.d["cc_fin"], ctx.d["cc_fout"]
    sLD, sPE, sACT, sDV, sPO, sST, sCC = (ctx.S("ld"), ctx.S("pe"), ctx.S("act"),
                                          ctx.S("dv"), ctx.S("po"), ctx.S("st"),
                                          ctx.S("cc"))
    with contextlib.ExitStack() as st:
        fg = ctx.sb(st, "rs_fg", [128, 128], F32)
        fgt = ctx.sb(st, "rs_fgt", [128, 128], F32)
        ident32 = ctx.masks["id32"]

        tdcol = ctx.sb(st, "rs_tdcol", [128, CH], F32)
        coefw = ctx.sb(st, "rs_coefw", [128, 64], F32)
        coefm = ctx.sb(st, "rs_coefm", [128, 64], F32)
        pws = ctx.sb(st, "rs_pws", [128, 8], F32)
        msks = ctx.sb(st, "rs_msks", [128, 64], F32)
        tmp8 = ctx.sb(st, "rs_tmp8", [128, 8], F32)
        scols = ctx.sb(st, "rs_scols", [128, 16], F32)
        ptA = ctx.ps(st, "rs_ptA", [128, 128], F32)
        pcoefe = ctx.ps(st, "rs_pcoefe", [128, 64], F32)

        with nc.Block("rwkv_s") as blk:
            def cc(g):
                sCC.inc(g.collective_compute(
                    "AllGather", ALU.bypass, replica_groups=RG,
                    ins=[cc_fin[:, :].opt()], outs=[cc_fout[:, :].opt()]), 1)
            blk.gpsimd(cc)

            def ld(s):
                sCC.wait(s)
                sLD.inc(s.dma_start(out=fg[:], in_=cc_fout[:, :]), 16)
                sLD.inc(s.dma_start(out=pws[:], in_=ctx.d["pw"][:, :]), 16)
                sLD.inc(s.dma_start(out=msks[:], in_=ctx.d["msk"][:, :]), 16)
                sLD.inc(s.dma_start(out=tdcol[:], in_=colv(ctx.d["time_decay"])), 16)
            blk.sync(ld)

            blk.gpsimd(lambda g: emit_identity(g, ident32, sPO))

            def pe(t):
                sLD.wait(t)
                sPO.wait(t)
                t.transpose(ptA[:], fg[:], ident32[:])
                t.drain().then_inc(sPE.h, 1)
                sPE.n += 1
            blk.tensor(pe)

            def dv1(v):
                sPE.wait(v)
                sDV.inc(v.tensor_copy(out=fgt[:], in_=ptA[:]), 1)
            blk.vector(dv1)

            def dbg_dump(s):
                sDV.wait(s)
                sST.inc(s.dma_start(out=ctx.d["h_fg"][:, :], in_=fg[:]), 16)
                sST.inc(s.dma_start(out=ctx.d["h_fgt"][:, :], in_=fgt[:]), 16)
            blk.sync(dbg_dump)

            def dv2(v):
                sDV.wait(v)
                for ch in range(CH):
                    v.tensor_scalar_mul(coefw[:, ch * 8:(ch + 1) * 8], pws[:],
                                        tdcol[:, ch:ch + 1])
                dinc(v, sDV)
            blk.vector(dv2)

            def a1(a):
                sDV.wait(a)
                a.activation(out=pcoefe[:], in_=coefw[:], func=AF.Exp)
                dinc(a, sACT)
            blk.scalar(a1)

            def dv3(v):
                sACT.wait(v)
                v.tensor_mul(coefm[:], pcoefe[:], msks[:])
                v.drain()
                for ch in range(CH):
                    v.tensor_mul(tmp8[:], coefm[:, ch * 8:(ch + 1) * 8],
                                 fgt[:, ch:ch + 113:16])
                    v.drain()
                    v.reduce_sum(out=scols[:, ch:ch + 1], in_=tmp8[:], axis=AX.X)
                    v.tensor_mul(tmp8[:], coefm[:, ch * 8:(ch + 1) * 8],
                                 fgt[:, 8 + ch:8 + ch + 113:16])
                    v.drain()
                    v.reduce_sum(out=scols[:, 8 + ch:9 + ch], in_=tmp8[:], axis=AX.X)
                dinc(v, sDV)
            blk.vector(dv3)

            def dbg2(s):
                pass
            def stq(s):
                sDV.wait(s)
                sST.inc(s.dma_start(out=ctx.d["h_scols"][:, :], in_=scols[:]), 16)

                sST.wait(s)
            blk.sync(stq)


def phase_rwkv_b(ctx):
    nc = ctx.nc
    h_num, h_den, h_sigr, h_s0 = (ctx.d["h_num"], ctx.d["h_den"],
                                  ctx.d["h_sigr"], ctx.d["h_s0"])
    sLD, sPE, sACT, sDV, sPO, sST = (ctx.S("ld"), ctx.S("pe"), ctx.S("act"),
                                     ctx.S("dv"), ctx.S("po"), ctx.S("st"))
    with contextlib.ExitStack() as st:
        woT = ctx.sb(st, "rb_woT", [128, CH * C], BF16)
        srt = ctx.sb(st, "rb_srt", [128, CH * TL], BF16)
        u16 = ctx.sb(st, "rb_u16", [128, CH * TL], BF16)
        nb = [ctx.sb(st, f"rb_nb{i}", [128, TL], F32) for i in range(2)]
        db = [ctx.sb(st, f"rb_db{i}", [128, TL], F32) for i in range(2)]
        t2 = ctx.sb(st, "rb_t2", [128, TL], F32)
        t3 = ctx.sb(st, "rb_t3", [128, TL], F32)
        sev = ctx.sb(st, "rb_sev", [128, TL], F32)
        iot = ctx.sb(st, "rb_iot", [128, TL], F32)
        tdcol = ctx.sb(st, "rb_tdcol", [128, CH], F32)
        eps8 = ctx.sb(st, "rb_eps8", [128, 1], F32)
        scols = ctx.sb(st, "rb_scols", [128, 16], F32)
        po = [ctx.ps(st, f"rb_po{i}", [128, TT], F32) for i in range(2)]

        with nc.Block("rwkv_b") as blk:
            def init(s):
                sLD.inc(s.dma_start(
                    out=woT[:].rearrange("p (ci c) -> p ci c", ci=CH),
                    in_=ctx.d["h_woT"].rearrange("(ci p) c -> p ci c", p=128)), 16)
                sLD.inc(s.dma_start(
                    out=srt[:].rearrange("p (ch t) -> p ch t", ch=CH),
                    in_=fmv(h_sigr)), 16)
                sLD.inc(s.dma_start(out=scols[:], in_=ctx.d["h_scols"][:, :]), 16)
                sLD.inc(s.dma_start(out=tdcol[:], in_=colv(ctx.d["time_decay"])), 16)
            blk.sync(init)

            def init_g(g):
                g.iota(iot[:], pattern=[[1, TL]], base=1, channel_multiplier=0,
                       allow_small_or_imprecise_dtypes=True)
                g.memset(eps8[:], 1e-8)
                dinc(g, sPO)
            blk.gpsimd(init_g)

            for ch in range(CH):
                nbuf, dbuf = nb[ch % 2], db[ch % 2]

                def ldc(s, ch=ch, nbuf=nbuf, dbuf=dbuf):
                    if ch >= 2:
                        sDV.wait(s)
                    sLD.inc(s.dma_start(out=nbuf[:],
                                        in_=h_num[ch * 128:(ch + 1) * 128, :]), 16)
                    sLD.inc(s.dma_start(out=dbuf[:],
                                        in_=h_den[ch * 128:(ch + 1) * 128, :]), 16)
                blk.sync(ldc)

                def acts(a, ch=ch):
                    sLD.wait(a)
                    sPO.wait(a)
                    sDV.wait(a)
                    a.activation(out=t2[:], in_=iot[:], func=AF.Exp,
                                 scale=tdcol[:, ch:ch + 1])
                    dinc(a, sACT)
                blk.scalar(acts)

                def dvc(v, ch=ch, nbuf=nbuf, dbuf=dbuf):
                    sACT.wait(v)
                    v.scalar_tensor_tensor(out=nbuf[:], in0=t2[:],
                                           scalar=scols[:, ch:ch + 1], in1=nbuf[:],
                                           op0=ALU.mult, op1=ALU.add)
                    sDV.inc(v.scalar_tensor_tensor(
                        out=dbuf[:], in0=t2[:], scalar=scols[:, 8 + ch:9 + ch],
                        in1=dbuf[:], op0=ALU.mult, op1=ALU.add), 1)
                blk.vector(dvc)

                def act2(a, ch=ch, dbuf=dbuf):
                    sDV.wait(a)
                    sPO.wait(a)
                    a.activation(out=t3[:], in_=dbuf[:], func=AF.Ln, bias=eps8[:, 0:1])
                    a.activation(out=t2[:], in_=t3[:], func=AF.Exp, scale=-1.0)
                    dinc(a, sACT)
                blk.scalar(act2)

                def dvw(v, ch=ch, nbuf=nbuf):
                    sACT.wait(v)
                    v.tensor_mul(t3[:], nbuf[:], t2[:])
                    sDV.inc(v.tensor_mul(u16[:, ch * TL:(ch + 1) * TL], t3[:],
                                         srt[:, ch * TL:(ch + 1) * TL]), 1)
                blk.vector(dvw)

            for co in range(CH):
                for tt in range(NT):
                    def pe(t, co=co, tt=tt):
                        sDV.wait(t)
                        bank = po[(co * NT + tt) % 2]
                        for ci in range(CH):
                            t.matmul(bank[:],
                                     woT[:, ci * C + co * 128: ci * C + co * 128 + 128],
                                     u16[:, ci * TL + tt * TT: ci * TL + tt * TT + TT],
                                     start=(ci == 0), stop=(ci == CH - 1))
                        t.drain().then_inc(sPE.h, 1)
                        sPE.n += 1
                    blk.tensor(pe)

                    def dve(v, co=co, tt=tt):
                        sPE.wait(v)
                        sST.wait(v)
                        sDV.inc(v.tensor_copy(out=sev[:, tt * TT:(tt + 1) * TT],
                                              in_=po[(co * NT + tt) % 2][:]), 1)
                    blk.vector(dve)

                def stc(s, co=co):
                    sDV.wait(s)
                    sST.inc(s.dma_start(out=h_s0[co * 128:(co + 1) * 128, :],
                                        in_=sev[:, 0:TL]), 16)
                blk.sync(stc)
            blk.sync(lambda s: sST.wait(s))




def phase_ag_xn0(ctx):
    nc = ctx.nc
    sCC = ctx.S("cc")
    with nc.Block("ag_xn0") as blk:
        def cc(g):
            sCC.inc(g.collective_compute(
                "AllGather", ALU.bypass, replica_groups=RG,
                ins=[ctx.d["h_xn16"][:, :].opt()],
                outs=[ctx.d["cc_x0"][:, :].opt()]), 1)
            g.wait_ge(sCC.h, sCC.n)
        blk.gpsimd(cc)


def phase_ffn0(ctx):
    nc = ctx.nc
    cc_x0, h_f0p = ctx.d["cc_x0"], ctx.d["h_f0p"]
    NTQ = (NCORE * TL) // TT
    sLD, sPE, sACT, sDV, sPO, sST = (ctx.S("ld"), ctx.S("pe"), ctx.S("act"),
                                     ctx.S("dv"), ctx.S("po"), ctx.S("st"))
    vg = ctx.d["wg_e"].rearrange("(ci p) f -> p ci f", p=128)
    vu = ctx.d["wu_e"].rearrange("(ci p) f -> p ci f", p=128)
    vd = ctx.d["wd_e"].rearrange("(sub p) c -> p sub c", p=128)
    for half in range(2):
        f0 = half * 2048
        with contextlib.ExitStack() as st:
            wg = ctx.sb(st, f"f0_wg{half}", [128, CH * 2048], BF16)
            wu = ctx.sb(st, f"f0_wu{half}", [128, CH * 2048], BF16)
            wd = ctx.sb(st, f"f0_wd{half}", [128, 16 * C], BF16)
            xb = [ctx.sb(st, f"f0_xb{half}{i}", [128, CH * TT], BF16) for i in range(2)]
            hsil = [ctx.sb(st, f"f0_hs{half}{i}", [128, 512], F32) for i in range(2)]
            h16 = ctx.sb(st, f"f0_h16{half}", [128, 16 * 512], BF16)
            dnv = [ctx.sb(st, f"f0_dn{half}{i}", [128, 512], F32) for i in range(4)]
            pg = [ctx.ps(st, f"f0_pg{half}{i}", [128, 512], F32) for i in range(2)]
            pu = [ctx.ps(st, f"f0_pu{half}{i}", [128, 512], F32) for i in range(2)]
            pa = [ctx.ps(st, f"f0_pa{half}{i}", [128, 512], F32) for i in range(4)]

            with nc.Block(f"ffn0_{half}") as blk:
                def wload(g, half=half):
                    sLD.inc(g.dma_start(
                        out=wg[:].rearrange("p (ci f) -> p ci f", ci=CH),
                        in_=vg[:, :, f0:f0 + 2048]), 16)
                    sLD.inc(g.dma_start(
                        out=wu[:].rearrange("p (ci f) -> p ci f", ci=CH),
                        in_=vu[:, :, f0:f0 + 2048]), 16)
                    sLD.inc(g.dma_start(
                        out=wd[:].rearrange("p (sub c) -> p sub c", sub=16),
                        in_=vd[:, half * 16:(half + 1) * 16, :]), 16)
                blk.gpsimd(wload)

                for tq in range(NTQ):
                    par = tq % 2

                    def ldx(s, tq=tq, par=par):
                        if tq >= 2:
                            s.wait_ge(sPE.h, xb_pe[tq - 2])
                        sLD.inc(s.dma_start(
                            out=xb[par][:].rearrange("p (ci t) -> p ci t", ci=CH),
                            in_=cc_x0.rearrange("(r ci p) t -> p r ci t", r=NCORE, p=128)
                            [:, (tq * TT) // TL, :, (tq * TT) % TL:(tq * TT) % TL + TT]), 16)
                    blk.sync(ldx)

                    xb_pe = {} if tq == 0 else xb_pe
                    dvh_marks = {}
                    for sub in range(16):
                        def pe_gu(t, par=par, sub=sub, tq=tq):
                            if sub == 0:
                                sLD.wait(t)
                            if sub >= 2:
                                t.wait_ge(sDV.h, dvh_marks[sub - 2])
                            for ci in range(CH):
                                t.matmul(pg[sub % 2][:],
                                         wg[:, ci * 2048 + sub * 128: ci * 2048 + sub * 128 + 128],
                                         xb[par][:, ci * TT:(ci + 1) * TT],
                                         start=(ci == 0), stop=(ci == CH - 1))
                            for ci in range(CH):
                                t.matmul(pu[sub % 2][:],
                                         wu[:, ci * 2048 + sub * 128: ci * 2048 + sub * 128 + 128],
                                         xb[par][:, ci * TT:(ci + 1) * TT],
                                         start=(ci == 0), stop=(ci == CH - 1))
                            t.drain().then_inc(sPE.h, 1)
                            sPE.n += 1
                            dvh_marks[(sub, "pe")] = sPE.n
                        blk.tensor(pe_gu)

                        def act_h(a, sub=sub):
                            a.wait_ge(sPE.h, dvh_marks[(sub, "pe")])
                            a.activation(out=hsil[sub % 2][:], in_=pg[sub % 2][:],
                                         func=AF.Silu)
                            dinc(a, sACT)
                            dvh_marks[(sub, "act")] = sACT.n
                        blk.scalar(act_h)

                        def dv_h(v, sub=sub):
                            v.wait_ge(sACT.h, dvh_marks[(sub, "act")])
                            v.tensor_mul(h16[:, sub * 512:(sub + 1) * 512],
                                         hsil[sub % 2][:], pu[sub % 2][:])
                            dinc(v, sDV)
                            dvh_marks[sub] = sDV.n
                        blk.vector(dv_h)

                    dn_marks = {}
                    for grp in range(2):
                        for cg in range(4):
                            co = grp * 4 + cg

                            def pe_dn(t, co=co, cg=cg, grp=grp, tq=tq):
                                if grp == 0 and cg == 0:
                                    t.wait_ge(sDV.h, dvh_marks[15])
                                if grp == 1:
                                    t.wait_ge(sDV.h, dn_marks[cg])
                                for sub in range(16):
                                    t.matmul(pa[cg][:],
                                             wd[:, sub * C + co * 128: sub * C + co * 128 + 128],
                                             h16[:, sub * 512:(sub + 1) * 512],
                                             start=(sub == 0), stop=(sub == 15))
                                t.drain().then_inc(sPE.h, 1)
                                sPE.n += 1
                                dn_marks[(co, "pe")] = sPE.n
                            blk.tensor(pe_dn)

                            def dv_dn(v, co=co, cg=cg, tq=tq):
                                v.wait_ge(sPE.h, dn_marks[(co, "pe")])
                                if True:
                                    sST.wait(v)
                                v.tensor_copy(out=dnv[cg][:], in_=pa[cg][:])
                                dinc(v, sDV)
                                dn_marks[cg] = sDV.n
                            blk.vector(dv_dn)

                            def st_dn(g, co=co, cg=cg, tq=tq, half=half):
                                g.wait_ge(sDV.h, dn_marks[cg])
                                rank = (tq * TT) // TL
                                col = (tq * TT) % TL
                                row = rank * C + co * 128
                                if half == 0:
                                    sST.inc(g.dma_start(
                                        out=h_f0p[row:row + 128, col:col + TT],
                                        in_=dnv[cg][:]), 16)
                                else:
                                    sST.inc(g.dma_start(
                                        out=h_f0p[row:row + 128, col:col + TT],
                                        in_=dnv[cg][:], accum_op=ALU.add), 16)
                            blk.gpsimd(st_dn)
                    xb_pe[tq] = sPE.n
                blk.sync(lambda s: sST.wait(s))


def phase_rs_f0(ctx):
    nc = ctx.nc
    sLD, sACT, sST, sCC = (ctx.S("ld"), ctx.S("act"), ctx.S("st"), ctx.S("cc"))
    with contextlib.ExitStack() as st:
        fb = [ctx.sb(st, f"rsf_fb{i}", [128, TT], F32) for i in range(2)]
        gb = [ctx.sb(st, f"rsf_gb{i}", [128, TT], F32) for i in range(2)]
        with nc.Block("rs_f0") as blk:
            def cc(g):
                sCC.inc(g.collective_compute(
                    "ReduceScatter", ALU.add, replica_groups=RG,
                    ins=[ctx.d["h_f0p"][:, :].opt()],
                    outs=[ctx.d["h_f0r"][:, :].opt()]), 1)
            blk.gpsimd(cc)

            marks = {}
            for co in range(CH):
                for tt in range(NT):
                    i = co * NT + tt
                    par = i % 2

                    def ld(s, co=co, tt=tt, par=par, i=i):
                        sCC.wait(s)
                        if i >= 2:
                            s.wait_ge(sACT.h, marks[i - 2])
                        sLD.inc(s.dma_start(
                            out=fb[par][:],
                            in_=ctx.d["h_f0r"][co * 128:(co + 1) * 128,
                                               tt * TT:(tt + 1) * TT]), 16)
                    blk.sync(ld)

                    def act(a, par=par, i=i):
                        sLD.wait(a)
                        if i >= 2:
                            sST.wait(a)
                        a.activation(out=gb[par][:], in_=fb[par][:], func=AF.Silu)
                        dinc(a, sACT)
                        marks[i] = sACT.n
                    blk.scalar(act)

                    def stg(s, co=co, tt=tt, par=par, i=i):
                        s.wait_ge(sACT.h, marks[i])
                        sST.inc(s.dma_start(
                            out=ctx.d["h_g0"][co * 128:(co + 1) * 128,
                                              tt * TT:(tt + 1) * TT],
                            in_=gb[par][:]), 16)
                    blk.sync(stg)
            blk.sync(lambda s: sST.wait(s))


def phase_resid_ln1(ctx):
    nc = ctx.nc
    h_x, h_s0, h_g0 = ctx.d["h_x"], ctx.d["h_s0"], ctx.d["h_g0"]
    h_x1, h_xn1 = ctx.d["h_x1"], ctx.d["h_xn1"]
    sLD, sPE, sACT, sDV, sPO, sST = (ctx.S("ld"), ctx.S("pe"), ctx.S("act"),
                                     ctx.S("dv"), ctx.S("po"), ctx.S("st"))
    with contextlib.ExitStack() as st:
        xb = ctx.sb(st, "r1_xb", [128, CH * TT], F32)
        sb_ = ctx.sb(st, "r1_sb", [128, CH * TT], F32)
        gb = ctx.sb(st, "r1_gb", [128, CH * TT], F32)
        x1t = ctx.sb(st, "r1_x1t", [128, CH * TT], F32)
        xn1t = ctx.sb(st, "r1_xn1t", [128, CH * TT], F32)
        xn1t16 = ctx.sb(st, "r1_xn1t16", [128, CH * TT], BF16)
        sq = ctx.sb(st, "r1_sq", [128, CH * TT], F32)
        ones_col = ctx.sb(st, "r1_ones_col", [128, 1], F32)
        ones_row = ctx.sb(st, "r1_ones_row", [1, 128], F32)
        rows = ctx.sb(st, "r1_rows", [1, 4 * TT], F32)
        epsc = ctx.sb(st, "r1_eps", [1, 1], F32)
        prs = ctx.ps(st, "r1_prs", [1, 512], F32)
        prq = ctx.ps(st, "r1_prq", [1, 512], F32)
        pbm = ctx.ps(st, "r1_pbm", [128, 512], F32)
        pbr = ctx.ps(st, "r1_pbr", [128, 512], F32)
        M0, R0, SC1, SC2 = 0, TT, 2 * TT, 3 * TT

        with nc.Block("resid_ln1") as blk:
            def init_v(v):
                v.memset(ones_col[:], 1.0)
                v.memset(epsc[:], float(LN_EPS))
                dinc(v, sDV)
                v.memset(ones_row[:], 1.0)
                dinc(v, sDV)
            blk.vector(init_v)

            for tt in range(NT):
                def ld(s, tt=tt):
                    if tt >= 1:
                        sACT.wait(s)
                        sPO.wait(s)
                        sDV.wait(s)
                    sLD.inc(s.dma_start(
                        out=xb[:].rearrange("p (ch t) -> p ch t", ch=CH),
                        in_=fmv(h_x)[:, :, PAD + tt * TT:PAD + (tt + 1) * TT]), 16)
                    sLD.inc(s.dma_start(
                        out=sb_[:].rearrange("p (ch t) -> p ch t", ch=CH),
                        in_=fmv(h_s0)[:, :, tt * TT:(tt + 1) * TT]), 16)
                    sLD.inc(s.dma_start(
                        out=gb[:].rearrange("p (ch t) -> p ch t", ch=CH),
                        in_=fmv(h_g0)[:, :, tt * TT:(tt + 1) * TT]), 16)
                blk.sync(ld)

                def dv_x1(v, tt=tt):
                    sLD.wait(v)
                    for ch in range(CH):
                        v.tensor_mul(x1t[:, ch * TT:(ch + 1) * TT],
                                     sb_[:, ch * TT:(ch + 1) * TT],
                                     gb[:, ch * TT:(ch + 1) * TT])
                    v.drain()
                    for ch in range(CH):
                        v.tensor_add(x1t[:, ch * TT:(ch + 1) * TT],
                                     x1t[:, ch * TT:(ch + 1) * TT],
                                     xb[:, ch * TT:(ch + 1) * TT])
                    dinc(v, sDV)
                blk.vector(dv_x1)

                def st_x1(s, tt=tt):
                    sDV.wait(s)
                    sST.inc(s.dma_start(
                        out=fmv(h_x1)[:, :, tt * TT:(tt + 1) * TT],
                        in_=x1t[:].rearrange("p (ch t) -> p ch t", ch=CH)), 16)
                blk.sync(st_x1)

                def pe_s(t, tt=tt):
                    sDV.wait(t)
                    for ch in range(CH):
                        t.matmul(prs[:], ones_col[:], x1t[:, ch * TT:(ch + 1) * TT],
                                 start=(ch == 0), stop=(ch == CH - 1))
                    t.drain().then_inc(sPE.h, 1)
                    sPE.n += 1
                blk.tensor(pe_s)

                def act_sq(a, tt=tt):
                    sDV.wait(a)
                    for ch in range(CH):
                        a.activation(out=sq[:, ch * TT:(ch + 1) * TT],
                                     in_=x1t[:, ch * TT:(ch + 1) * TT], func=AF.Square)
                    dinc(a, sACT)
                blk.scalar(act_sq)

                def pe_sq(t, tt=tt):
                    sACT.wait(t)
                    for ch in range(CH):
                        t.matmul(prq[:], ones_col[:], sq[:, ch * TT:(ch + 1) * TT],
                                 start=(ch == 0), stop=(ch == CH - 1))
                    t.drain().then_inc(sPE.h, 1)
                    sPE.n += 1
                blk.tensor(pe_sq)

                def dv_rows(v, tt=tt):
                    sPE.wait(v)
                    v.tensor_scalar_mul(rows[:, M0:M0 + TT], prs[:], 1.0 / C)
                    v.tensor_scalar_mul(rows[:, SC1:SC1 + TT], prq[:], 1.0 / C)
                    v.drain()
                    v.tensor_mul(rows[:, SC2:SC2 + TT], rows[:, M0:M0 + TT],
                                 rows[:, M0:M0 + TT])
                    v.drain()
                    v.tensor_sub(rows[:, SC1:SC1 + TT], rows[:, SC1:SC1 + TT],
                                 rows[:, SC2:SC2 + TT])
                    dinc(v, sDV)
                blk.vector(dv_rows)

                def act_rstd(a, tt=tt):
                    sDV.wait(a)
                    a.activation(out=rows[:, SC2:SC2 + TT], in_=rows[:, SC1:SC1 + TT],
                                 func=AF.Ln, bias=epsc[0:1, 0:1])
                    a.drain()
                    a.activation(out=rows[:, R0:R0 + TT], in_=rows[:, SC2:SC2 + TT],
                                 func=AF.Exp, scale=-0.5)
                    dinc(a, sACT)
                blk.scalar(act_rstd)

                def pe_b(t, tt=tt):
                    sACT.wait(t)
                    sDV.wait(t)
                    t.matmul(pbm[:], ones_row[:], rows[0:1, M0:M0 + TT],
                             start=True, stop=True)
                    t.matmul(pbr[:], ones_row[:], rows[0:1, R0:R0 + TT],
                             start=True, stop=True)
                    t.drain().then_inc(sPE.h, 1)
                    sPE.n += 1
                blk.tensor(pe_b)

                def dv_app(v, tt=tt):
                    sPE.wait(v)
                    for ch in range(CH):
                        v.scalar_tensor_tensor(
                            out=xn1t[:, ch * TT:(ch + 1) * TT], in0=pbm[:],
                            scalar=-1.0, in1=x1t[:, ch * TT:(ch + 1) * TT],
                            op0=ALU.mult, op1=ALU.add)
                        v.tensor_mul(xn1t[:, ch * TT:(ch + 1) * TT],
                                     xn1t[:, ch * TT:(ch + 1) * TT], pbr[:])
                    dinc(v, sDV)
                blk.vector(dv_app)

                def po_c(g, tt=tt):
                    sDV.wait(g)
                    if tt >= 1:
                        sST.wait(g)
                    for ch in range(CH):
                        g.tensor_copy(out=xn1t16[:, ch * TT:(ch + 1) * TT],
                                      in_=xn1t[:, ch * TT:(ch + 1) * TT])
                    dinc(g, sPO)
                blk.gpsimd(po_c)

                def sto(s, tt=tt):
                    sPO.wait(s)
                    sST.inc(s.dma_start(
                        out=fmv(h_xn1)[:, :, tt * TT:(tt + 1) * TT],
                        in_=xn1t16[:].rearrange("p (ch t) -> p ch t", ch=CH)), 16)
                    if "h_xn1_dbg" in ctx.d:
                        sST.inc(s.dma_start(
                            out=fmv(ctx.d["h_xn1_dbg"])[:, :, tt * TT:(tt + 1) * TT],
                            in_=xn1t16[:].rearrange("p (ch t) -> p ch t", ch=CH)), 16)
                blk.sync(sto)
            blk.sync(lambda s: sST.wait(s))


def phase_ag_xn1(ctx):
    nc = ctx.nc
    sCC = ctx.S("cc")
    with nc.Block("ag_xn1") as blk:
        def cc(g):
            sCC.inc(g.collective_compute(
                "AllGather", ALU.bypass, replica_groups=RG,
                ins=[ctx.d["h_xn1"][:, :].opt()],
                outs=[ctx.d["cc_xg"][:, :].opt()]), 1)
            g.wait_ge(sCC.h, sCC.n)
        blk.gpsimd(cc)




def emit_wT(ctx, blk, bufs, src, dst, R, CC):
    (wrow, wrow16, ident16, evac, pt) = bufs
    sLD, sPE, sDV, sPO, sST = (ctx.S("ld"), ctx.S("pe"), ctx.S("dv"),
                               ctx.S("po"), ctx.S("st"))
    NR = (R + 127) // 128
    NCC = CC // 128
    for ro in range(NR):
        rw = min(128, R - ro * 128)

        def ld(s, ro=ro, rw=rw):
            sPO.wait(s)
            sLD.inc(s.dma_start(out=wrow[0:rw, 0:CC],
                                in_=src[ro * 128:ro * 128 + rw, :]), 16)
        blk.sync(ld)

        def cast(g, rw=rw):
            sLD.wait(g)
            sST.wait(g)
            g.tensor_copy(out=wrow16[0:rw, 0:CC], in_=wrow[0:rw, 0:CC])
            dinc(g, sPO)
        blk.gpsimd(cast)

        dv_base = sDV.n
        pe_base = sPE.n

        def pe(t, rw=rw, dv_base=dv_base, NCC=NCC):
            sPO.wait(t)
            sST.wait(t)
            for ci in range(NCC):
                if ci >= 2:
                    t.wait_ge(sDV.h, dv_base + ci - 1)
                t.transpose(pt[ci % 2][:, 0:rw],
                            wrow16[0:rw, ci * 128:(ci + 1) * 128],
                            ident16[0:rw, 0:rw])
                t.drain().then_inc(sPE.h, 1)
                sPE.n += 1
        blk.tensor(pe)

        def dv(v, rw=rw, pe_base=pe_base, NCC=NCC):
            for ci in range(NCC):
                v.wait_ge(sPE.h, pe_base + ci + 1)
                sDV.inc(v.tensor_copy(out=evac[:, ci * 128:ci * 128 + rw],
                                      in_=pt[ci % 2][:, 0:rw]), 1)
        blk.vector(dv)

        def stq(s, ro=ro, rw=rw, NCC=NCC):
            sDV.wait(s)
            sST.inc(s.dma_start(
                out=dst.rearrange("(ci p) r -> p ci r", p=128)[:, :, ro * 128:ro * 128 + rw],
                in_=evac[:, 0:NCC * 128].rearrange("p (ci r) -> p ci r", ci=NCC)[:, :, 0:rw]), 16)
        blk.sync(stq)


def phase_prep_w1(ctx):
    nc = ctx.nc
    with contextlib.ExitStack() as st:
        wrow = ctx.sb(st, "p1_row", [128, 2048], F32)
        wrow16 = ctx.sb(st, "p1_row16", [128, 2048], BF16)
        ident16 = ctx.masks["id16"]
        evac = ctx.sb(st, "p1_evac", [128, 2048], BF16)
        pt = [ctx.ps(st, f"p1_pt{i}", [128, 512], BF16) for i in range(2)]
        bufs = (wrow, wrow16, ident16, evac, pt)
        with nc.Block("prep_w1") as blk:
            emit_wT(ctx, blk, bufs, ctx.d["wqkv_s"], ctx.d["h_wqkvT"], 384, C)
            blk.sync(lambda s: ctx.S("st").wait(s))


NQ = T // TT
NBK = T // 128


def phase_l1_attn(ctx, hb):
    sfx = f"_{hb}"
    import os
    parts = os.environ.get("ATT_PARTS", "qkv,lh,main").split(",")
    smax = int(os.environ.get("ATT_SMAX", str(NQ)))
    dbgS = int(os.environ.get("ATT_DBGS", "-1"))
    safe = set(os.environ.get("ATT_SAFE", "").split(","))

    def SAFE(eng, tag):
        if "1" in safe or tag in safe:
            ctx.S("ld").wait(eng)
            ctx.S("pe").wait(eng)
            ctx.S("act").wait(eng)
            ctx.S("dv").wait(eng)
            ctx.S("po").wait(eng)
    nc = ctx.nc
    sLD, sPE, sACT, sDV, sPO, sST = (ctx.S("ld"), ctx.S("pe"), ctx.S("act"),
                                     ctx.S("dv"), ctx.S("po"), ctx.S("st"))
    cc_xg = ctx.d["cc_xg"]
    lhc = ctx.d["h_lhc"]
    h_out = ctx.d["h_out"]

    with contextlib.ExitStack() as st:
        vuT = ctx.sb(st, "a1_vuT" + sfx, [64, T], BF16)
        qcT = ctx.sb(st, "a1_qcT" + sfx, [64, T], BF16)
        kcT = ctx.sb(st, "a1_kcT" + sfx, [64, T], BF16)
        vcT = ctx.sb(st, "a1_vcT" + sfx, [64, T], BF16)
        vctm = ctx.sb(st, "a1_vctm" + sfx, [128, NBK * 64], BF16)
        ident16 = ctx.masks["id16"]
        ident32 = ctx.masks["id32"]
        cmask = ctx.masks["cmask"]
        t1mask = ctx.masks["t1mask"]
        lhmask = ctx.masks["lhmask"]
        outb = ctx.sb(st, "a1_outb" + sfx, [64, T], BF16)
        psu = [ctx.ps(st, f"a1_psu{i}" + sfx, [128, 512], F32) for i in range(4)]
        pt1 = ctx.ps(st, "a1_pt1" + sfx, [128, 512], F32)
        ptb = ctx.ps(st, "a1_ptb" + sfx, [128, 512], BF16)
        pav = ctx.ps(st, "a1_pav" + sfx, [128, 64], F32)
        pal = ctx.ps(st, "a1_pal" + sfx, [128, 512], F32)

        with contextlib.ExitStack() as st2:
            quT = ctx.sb(st2, "a1_quT" + sfx, [64, T], BF16)
            kuT = ctx.sb(st2, "a1_kuT" + sfx, [64, T], BF16)
            wqkvT = ctx.sb(st2, "a1_wqkvT" + sfx, [128, CH * 384], BF16)
            xgb = [ctx.sb(st2, f"a1_xgb{i}" + sfx, [128, CH * TT], BF16) for i in range(2)]
            lhw = [ctx.sb(st2, f"a1_lhw{i}" + sfx, [128, 512], BF16) for i in range(2)]
            qkv_t = [quT, kuT, vuT, qcT, kcT, vcT]

            with (nc.Block(f"l1qkv{hb}") if "qkv" in parts else _NullBlock()) as blk:
                def init(s):
                    sLD.inc(s.dma_start(
                        out=wqkvT[:].rearrange("p (ci d) -> p ci d", ci=CH),
                        in_=ctx.d["h_wqkvT"].rearrange("(ci p) d -> p ci d", p=128)), 16)
                blk.sync(init)

                pe_mark = {}
                act_mark = {}
                for qt in range(NQ):
                    par = qt % 2
                    rank = hb * 4 + (qt * TT) // TL
                    col0 = (qt * TT) % TL

                    def ldx(s, par=par, rank=rank, col0=col0, qt=qt):
                        if qt >= 2:
                            s.wait_ge(sPE.h, pe_mark[(qt - 2, 5)])
                        sLD.inc(s.dma_start(
                            out=xgb[par][:].rearrange("p (ci t) -> p ci t", ci=CH),
                            in_=cc_xg.rearrange("(r ci p) t -> p r ci t", r=NCORE, p=128)
                            [:, rank, :, col0:col0 + TT]), 16)
                    blk.sync(ldx)

                    for q6 in range(6):
                        def pe_q(t, par=par, qt=qt, q6=q6):
                            if q6 == 0:
                                sLD.wait(t)
                                sPO.wait(t)
                            if qt * 6 + q6 >= 4:
                                k = qt * 6 + q6 - 4
                                t.wait_ge(sACT.h, act_mark[(k // 6, k % 6)])
                            bank = psu[q6 % 4]
                            for ci in range(CH):
                                t.matmul(bank[0:64, :],
                                         wqkvT[:, ci * 384 + q6 * 64: ci * 384 + q6 * 64 + 64],
                                         xgb[par][:, ci * TT:(ci + 1) * TT],
                                         start=(ci == 0), stop=(ci == CH - 1))
                            t.drain().then_inc(sPE.h, 1)
                            sPE.n += 1
                            pe_mark[(qt, q6)] = sPE.n
                        blk.tensor(pe_q)

                        def act_q(a, qt=qt, q6=q6):
                            a.wait_ge(sPE.h, pe_mark[(qt, q6)])
                            scale = 0.125 if q6 in (0, 3) else 1.0
                            a.activation(out=qkv_t[q6][:, qt * TT:(qt + 1) * TT],
                                         in_=psu[q6 % 4][0:64, :], func=AF.Identity,
                                         scale=scale)
                            dinc(a, sACT)
                            act_mark[(qt, q6)] = sACT.n
                        blk.scalar(act_q)

                vt_pe = {}
                vt_dv = {}
                vtbank = [ptb, pav.bitcast(BF16)]
                for tb in range(NBK):
                    def pe_vt(t, tb=tb):
                        if tb == 0:
                            sACT.wait(t)
                        if tb >= 2:
                            t.wait_ge(sDV.h, vt_dv[tb - 2])
                        t.transpose(vtbank[tb % 2][:, 0:64],
                                    vcT[:, tb * 128:(tb + 1) * 128],
                                    ident16[0:64, 0:64])
                        t.drain().then_inc(sPE.h, 1)
                        sPE.n += 1
                        vt_pe[tb] = sPE.n
                    blk.tensor(pe_vt)

                    def dv_vt(v, tb=tb):
                        v.wait_ge(sPE.h, vt_pe[tb])
                        sDV.inc(v.tensor_copy(
                            out=vctm[:, tb * 64:(tb + 1) * 64],
                            in_=vtbank[tb % 2][:, 0:64]), 1)
                        vt_dv[tb] = sDV.n
                    blk.vector(dv_vt)

            with (nc.Block(f"l1lh{hb}") if "lh" in parts else _NullBlock()) as blk:
                lh_act = [0, 0]
                lh_st = [0, 0]
                seq = 0
                for J in range(NBK):
                    for kt in range(J // 4 + 1):
                        par = seq % 2
                        seq += 1
                        masked = (kt == J // 4)

                        def pe_lh(t, J=J, kt=kt, par=par):
                            t.wait_ge(sACT.h, lh_act[par])
                            t.matmul(psu[par][:],
                                     kuT[:, J * 128:(J + 1) * 128],
                                     quT[:, kt * 512:(kt + 1) * 512],
                                     start=True, stop=True)
                            t.drain().then_inc(sPE.h, 1)
                            sPE.n += 1
                        blk.tensor(pe_lh)
                        my_pe = sPE.n

                        def act_lh(a, par=par, my_pe=my_pe,
                                   myst=lh_st[par]):
                            a.wait_ge(sPE.h, my_pe)
                            a.wait_ge(sST.h, myst)
                            a.activation(out=lhw[par][:], in_=psu[par][:],
                                         func=AF.Sigmoid)
                            dinc(a, sACT)
                        blk.scalar(act_lh)
                        lh_act[par] = sACT.n
                        my_act = sACT.n

                        my_dv = sDV.n
                        if masked:
                            def dv_m(v, J=J, par=par, my_act=my_act):
                                v.wait_ge(sACT.h, my_act)
                                mi = J % 4
                                v.tensor_mul(lhw[par][:], lhw[par][:],
                                             lhmask[:, mi * 512:(mi + 1) * 512])
                                dinc(v, sDV)
                            blk.vector(dv_m)
                            my_dv = sDV.n

                        def st_lh(s, J=J, kt=kt, par=par, masked=masked,
                                  my_act=my_act, my_dv=my_dv):
                            if masked:
                                s.wait_ge(sDV.h, my_dv)
                            else:
                                s.wait_ge(sACT.h, my_act)
                            row = ((hb * 8 + kt // 2) * NBK + J) * 128
                            sST.inc(s.dma_start(
                                out=lhc[row:row + 128, (kt % 2) * 512:(kt % 2) * 512 + 512],
                                in_=lhw[par][:]), 16)
                        blk.sync(st_lh)
                        lh_st[par] = sST.n
                if dbgS >= 0 and hb == 0:
                    def st_lhdbg(s):
                        sST.wait(s)
                        sST.inc(s.dma_start(
                            out=ctx.d["h_lh_dbg"][:, :],
                            in_=lhc[0:16 * 128, :]), 16)
                blk.sync(st_lhdbg if (dbgS >= 0 and hb == 0) else (lambda s: sST.wait(s)))

        with contextlib.ExitStack() as st3:
            t1s = ctx.sb(st3, "a1_t1s" + sfx, [128, NBK * 512], BF16)
            lhb = [ctx.sb(st3, f"a1_lhb{i}" + sfx, [128, 8 * 512], BF16) for i in range(2)]
            scor = ctx.sb(st3, "a1_scor" + sfx, [128, 512], F32)
            ssil = ctx.sb(st3, "a1_ssil" + sfx, [128, 512], F32)
            p16 = ctx.sb(st3, "a1_p16" + sfx, [128, 512], BF16)
            pT16 = ctx.sb(st3, "a1_pT16" + sfx, [128, 512], BF16)
            mcol = ctx.sb(st3, "a1_mcol" + sfx, [128, 4], F32)
            lcol = ctx.sb(st3, "a1_lcol" + sfx, [128, 4], F32)
            tcol = ctx.sb(st3, "a1_tcol" + sfx, [128, 4], F32)
            oacc = ctx.sb(st3, "a1_oacc" + sfx, [128, 4 * 64], F32)

            with (nc.Block(f"l1at{hb}") if "main" in parts else _NullBlock()) as blk:
                prev_out_dv = 0
                prev_fin_pe = 0
                prev_ktpe = [0]
                for S in range(smax):
                    nJ = 4 * S + 4
                    t1_pe = {}
                    t1_dv = {}
                    po_t1_mark = [0]

                    t1banks = [pt1, pal]
                    for J in range(nJ):
                        def pe_t1(t, S=S, J=J, prev_out_dv=prev_out_dv):
                            if J == 0:
                                if S == 0:
                                    sDV.wait(t)
                                    sPO.wait(t)
                                else:
                                    t.wait_ge(sDV.h, prev_out_dv)
                            if J >= 2:
                                t.wait_ge(sDV.h, t1_dv[J - 2])
                            t.matmul(t1banks[J % 2][:],
                                     vuT[:, J * 128:(J + 1) * 128],
                                     qcT[:, S * TT:(S + 1) * TT],
                                     start=True, stop=True)
                            t.drain().then_inc(sPE.h, 1)
                            sPE.n += 1
                            t1_pe[J] = sPE.n
                        blk.tensor(pe_t1)

                        def dv_t1(v, S=S, J=J):
                            v.wait_ge(sPE.h, t1_pe[J])
                            if J >= 4 * S:
                                mi = J - 4 * S
                                sDV.inc(v.tensor_mul(
                                    t1s[:, J * 512:(J + 1) * 512],
                                    t1banks[J % 2][:],
                                    t1mask[:, mi * 512:(mi + 1) * 512]), 1)
                            else:
                                sDV.inc(v.tensor_copy(
                                    out=t1s[:, J * 512:(J + 1) * 512],
                                    in_=t1banks[J % 2][:]), 1)
                            t1_dv[J] = sDV.n
                        blk.vector(dv_t1)
                    po_t1_mark[0] = sPO.n

                    if S == dbgS and hb == 0:
                        def st_t1dbg(s, S=S):
                            sDV.wait(s)
                            sPO.wait(s)
                            ctx.S("st").inc(s.dma_start(
                                out=ctx.d["h_t1_dbg"][:, :],
                                in_=t1s[:, 0:(4 * S + 4) * 512]), 16)
                        blk.sync(st_t1dbg)

                    def dv_fi(v, S=S, prev_fin_pe=prev_fin_pe):
                        SAFE(v, "dv_fi")
                        if S > 0:
                            v.wait_ge(sPE.h, prev_fin_pe)
                        v.memset(mcol[:], -3.0e38)
                        v.memset(lcol[:], 0.0)
                        v.memset(oacc[:], 0.0)
                        dinc(v, sDV)
                    blk.vector(dv_fi)
                    fi_mark = sDV.n

                    dvsc_marks = [fi_mark]
                    prev_pt = [0]
                    for kt in range(S + 1):
                        J0 = 4 * kt
                        njt = nJ - J0
                        nbat = (njt + 7) // 8
                        lh_ld = {}
                        bt_pe = {}

                        for bt in range(nbat):
                            def ld_lh(s, S=S, kt=kt, bt=bt, J0=J0, njt=njt,
                                      pk=prev_ktpe[0]):
                                SAFE(s, "ld_lh")
                                if bt >= 2:
                                    s.wait_ge(sPE.h, bt_pe[bt - 2])
                                else:
                                    s.wait_ge(sPE.h, pk)
                                nj = min(8, njt - bt * 8)
                                row = ((hb * 8 + kt // 2) * NBK + (J0 + bt * 8)) * 128
                                sLD.inc(s.dma_start(
                                    out=lhb[bt % 2][:, 0:nj * 512].rearrange(
                                        "p (j k) -> p j k", j=nj),
                                    in_=lhc[row:row + nj * 128,
                                            (kt % 2) * 512:(kt % 2) * 512 + 512]
                                    .rearrange("(j p) k -> p j k", p=128)), 16)
                                lh_ld[bt] = sLD.n
                            blk.sync(ld_lh)

                            def pe_bt(t, S=S, kt=kt, bt=bt, J0=J0, njt=njt, nJ=nJ):
                                SAFE(t, "pe_bt")
                                t.wait_ge(sLD.h, lh_ld[bt])
                                if bt == 0:
                                    t.wait_ge(sPO.h, po_t1_mark[0])
                                    t.wait_ge(sDV.h, dvsc_marks[-1])
                                nj = min(8, njt - bt * 8)
                                for isub in range(4):
                                    iblk = 4 * S + isub
                                    for j in range(nj):
                                        J = J0 + bt * 8 + j
                                        if J > iblk:
                                            continue
                                        t.matmul(psu[isub][:],
                                                 t1s[:, J * 512 + isub * 128: J * 512 + isub * 128 + 128],
                                                 lhb[bt % 2][:, j * 512:(j + 1) * 512],
                                                 start=(J == J0), stop=(J == iblk))
                                t.drain().then_inc(sPE.h, 1)
                                sPE.n += 1
                                bt_pe[bt] = sPE.n
                            blk.tensor(pe_bt)

                        for isub in range(4):
                            iblk = 4 * S + isub

                            def act_sil(a, isub=isub, m=bt_pe, nbat=nbat,
                                        pm=None):
                                SAFE(a, "act_sil")
                                a.wait_ge(sPE.h, m[nbat - 1])
                                a.wait_ge(sDV.h, dvsc_marks[-1])
                                a.activation(out=ssil[:], in_=psu[isub][:],
                                             func=AF.Silu)
                                dinc(a, sACT)
                            blk.scalar(act_sil)
                            sil_mark = sACT.n

                            def pe_sc(t, isub=isub, iblk=iblk, kt=kt,
                                      sil_mark=sil_mark):
                                SAFE(t, "pe_sc")
                                t.wait_ge(sACT.h, sil_mark)
                                t.matmul(psu[isub][:],
                                         qcT[:, iblk * 128:(iblk + 1) * 128],
                                         kcT[:, kt * 512:(kt + 1) * 512],
                                         start=True, stop=True)
                                t.drain().then_inc(sPE.h, 1)
                                sPE.n += 1
                            blk.tensor(pe_sc)
                            sc_pe = sPE.n

                            need_mask = (kt * 512 + 512) > (iblk * 128 + 1)

                            def dv_sc(v, isub=isub, sc_pe=sc_pe, iblk=iblk,
                                      kt=kt, need_mask=need_mask):
                                SAFE(v, "dv_sc")
                                v.wait_ge(sPE.h, sc_pe)
                                sACT.wait(v)
                                v.tensor_sub(scor[:], psu[isub][:], ssil[:])
                                if need_mask:
                                    mi = (iblk * 128 - kt * 512) // 128
                                    v.drain()
                                    v.tensor_add(scor[:], scor[:],
                                                 cmask[:, mi * 512:(mi + 1) * 512])
                                dinc(v, sDV)
                            blk.vector(dv_sc)
                            dvsc_marks.append(sDV.n)
                            sc_mark = sDV.n

                            def dv_fl(v, isub=isub):
                                SAFE(v, "dv_fl")
                                v.reduce_max(out=tcol[:, 0:1], in_=scor[:], axis=AX.X)
                                v.drain()
                                v.tensor_max(tcol[:, 1:2], tcol[:, 0:1],
                                             mcol[:, isub:isub + 1])
                                v.drain()
                                v.tensor_sub(pal[:, 2:3], mcol[:, isub:isub + 1],
                                             tcol[:, 1:2])
                                v.tensor_copy(out=mcol[:, isub:isub + 1],
                                              in_=tcol[:, 1:2])
                                v.drain()
                                v.tensor_scalar(out=scor[:], in0=scor[:],
                                                scalar1=tcol[:, 1:2], scalar2=None,
                                                op0=ALU.subtract)
                                v.drain()
                                dinc(v, sDV)
                            blk.vector(dv_fl)
                            fl_mark = sDV.n

                            def act_exp(a, fl_mark=fl_mark, ppm=None):
                                SAFE(a, "act_exp")
                                a.wait_ge(sDV.h, fl_mark)
                                a.wait_ge(sPE.h, prev_pt[0])
                                a.activation(out=pal[:, 0:1], in_=pal[:, 2:3],
                                             func=AF.Exp)
                                a.drain()
                                a.activation(out=p16[:], in_=scor[:],
                                             func=AF.Exp,
                                             accum_out=pal[:, 1:2])
                                dinc(a, sACT)
                            blk.scalar(act_exp)
                            exp_mark = sACT.n

                            def dv_al(v, isub=isub, exp_mark=exp_mark):
                                SAFE(v, "dv_al")
                                v.wait_ge(sACT.h, exp_mark)
                                v.tensor_copy(out=tcol[:, 0:2], in_=pal[:, 0:2])
                                v.drain()
                                v.scalar_tensor_tensor(
                                    out=lcol[:, isub:isub + 1],
                                    in0=lcol[:, isub:isub + 1],
                                    scalar=tcol[:, 0:1], in1=tcol[:, 1:2],
                                    op0=ALU.mult, op1=ALU.add)
                                v.tensor_scalar_mul(oacc[:, isub * 64:(isub + 1) * 64],
                                                    oacc[:, isub * 64:(isub + 1) * 64],
                                                    tcol[:, 0:1])
                                dinc(v, sDV)
                            blk.vector(dv_al)
                            al_mark = sDV.n

                            def pe_pt(t, exp_mark=exp_mark, al_mark=al_mark):
                                SAFE(t, "pe_pt")
                                t.wait_ge(sACT.h, exp_mark)
                                t.wait_ge(sDV.h, al_mark)
                                for kb in range(4):
                                    t.transpose(ptb[:, kb * 128:(kb + 1) * 128],
                                                p16[:, kb * 128:(kb + 1) * 128],
                                                ident16[:])
                                t.drain().then_inc(sPE.h, 1)
                                sPE.n += 1
                            blk.tensor(pe_pt)
                            pt_mark = sPE.n
                            prev_pt[0] = pt_mark

                            def dv_pt(v, pt_mark=pt_mark):
                                SAFE(v, "dv_pt")
                                v.wait_ge(sPE.h, pt_mark)
                                v.tensor_copy(out=pT16[:], in_=ptb[:])
                                dinc(v, sDV)
                            blk.vector(dv_pt)
                            ptv_mark = sDV.n

                            def pe_av(t, kt=kt, ptv_mark=ptv_mark):
                                SAFE(t, "pe_av")
                                t.wait_ge(sDV.h, ptv_mark)
                                for kb in range(4):
                                    t.matmul(pav[:],
                                             pT16[:, kb * 128:(kb + 1) * 128],
                                             vctm[:, (kt * 4 + kb) * 64:(kt * 4 + kb) * 64 + 64],
                                             start=(kb == 0), stop=(kb == 3))
                                t.drain().then_inc(sPE.h, 1)
                                sPE.n += 1
                            blk.tensor(pe_av)
                            av_mark = sPE.n

                            def dv_av(v, isub=isub, av_mark=av_mark):
                                SAFE(v, "dv_av")
                                v.wait_ge(sPE.h, av_mark)
                                v.tensor_add(oacc[:, isub * 64:(isub + 1) * 64],
                                             oacc[:, isub * 64:(isub + 1) * 64],
                                             pav[:, 0:64])
                                dinc(v, sDV)
                            blk.vector(dv_av)
                        prev_ktpe[0] = sPE.n

                    def dv_fin(v, S=S):
                        SAFE(v, "dv_fin")
                        v.drain()
                        for isub in range(4):
                            v.reciprocal(tcol[:, isub:isub + 1],
                                         lcol[:, isub:isub + 1])
                        v.drain()
                        for isub in range(4):
                            v.tensor_scalar_mul(oacc[:, isub * 64:(isub + 1) * 64],
                                                oacc[:, isub * 64:(isub + 1) * 64],
                                                tcol[:, isub:isub + 1])
                        dinc(v, sDV)
                    blk.vector(dv_fin)
                    fin_mark = sDV.n

                    def pe_fin(t, S=S, fin_mark=fin_mark):
                        SAFE(t, "pe_fin")
                        t.wait_ge(sDV.h, fin_mark)
                        for isub in range(4):
                            t.transpose(pt1[0:64, isub * 128:(isub + 1) * 128],
                                        oacc[:, isub * 64:(isub + 1) * 64],
                                        ident32[:])
                        t.drain().then_inc(sPE.h, 1)
                        sPE.n += 1
                    blk.tensor(pe_fin)
                    prev_fin_pe = sPE.n

                    def dv_ot(v, S=S, prev_fin_pe=prev_fin_pe):
                        SAFE(v, "dv_ot")
                        v.wait_ge(sPE.h, prev_fin_pe)
                        sDV.inc(v.tensor_copy(
                            out=outb[:, 4 * S * 128:(4 * S + 4) * 128],
                            in_=pt1[0:64, 0:512]), 1)
                    blk.vector(dv_ot)
                    prev_out_dv = sDV.n

                def st_out(s):
                    sDV.wait(s)
                    sST.inc(s.dma_start(out=h_out[hb * 64:(hb + 1) * 64, :],
                                        in_=outb[:]), 16)
                    if "h_out_dbg" in ctx.d:
                        sST.inc(s.dma_start(
                            out=ctx.d["h_out_dbg"][hb * 64:(hb + 1) * 64, :],
                            in_=outb[:]), 16)
                    sST.wait(s)
                blk.sync(st_out)



def phase_s1p(ctx):
    nc = ctx.nc
    sLD, sPE, sDV, sST, sCC = (ctx.S("ld"), ctx.S("pe"), ctx.S("dv"),
                               ctx.S("st"), ctx.S("cc"))
    with contextlib.ExitStack() as st:
        wsT = ctx.sb(st, "s1_wsT", [64, C], BF16)
        ob = [ctx.sb(st, f"s1_ob{i}", [64, T], BF16) for i in range(2)]
        sev = [ctx.sb(st, f"s1_sev{i}", [128, TT], F32) for i in range(2)]
        pp = [ctx.ps(st, f"s1_pp{i}", [128, TT], F32) for i in range(2)]
        with nc.Block("s1p") as blk:
            def init(g):
                sST.wait(g)
                sLD.inc(g.dma_start(out=wsT[:], in_=ctx.d["wout_sT"][:, :]), 16)
            blk.gpsimd(init)

            def init2(s):
                for hb in range(2):
                    sLD.inc(s.dma_start(
                        out=ob[hb][:],
                        in_=ctx.d["h_out"][hb * 64:hb * 64 + 64, :]), 16)
            blk.sync(init2)

            pe_m = {}
            dv_m = {}
            st_m = {}
            for hb in range(2):
                for qt in range(NQ):
                    for co in range(CH):
                        i = (hb * NQ + qt) * CH + co
                        par = i % 2

                        def pe(t, hb=hb, qt=qt, co=co, i=i, par=par):
                            if i == 0:
                                sLD.wait(t)
                            if i >= 2:
                                t.wait_ge(sDV.h, dv_m[i - 2])
                            t.matmul(pp[par][:],
                                     wsT[:, co * 128:(co + 1) * 128],
                                     ob[hb][:, qt * TT:(qt + 1) * TT],
                                     start=True, stop=True)
                            t.drain().then_inc(sPE.h, 1)
                            sPE.n += 1
                            pe_m[i] = sPE.n
                        blk.tensor(pe)

                        def dv(v, i=i, par=par):
                            v.wait_ge(sPE.h, pe_m[i])
                            if i >= 2:
                                v.wait_ge(sST.h, st_m[i - 2])
                            v.tensor_copy(out=sev[par][:], in_=pp[par][:])
                            dinc(v, sDV)
                            dv_m[i] = sDV.n
                        blk.vector(dv)

                        def stq(s, hb=hb, qt=qt, co=co, i=i, par=par):
                            s.wait_ge(sDV.h, dv_m[i])
                            gt = qt * TT
                            rank = hb * 4 + gt // TL
                            col = gt % TL
                            sST.inc(s.dma_start(
                                out=ctx.d["h_s1p"][rank * C + co * 128:
                                                   rank * C + co * 128 + 128,
                                                   col:col + TT],
                                in_=sev[par][:]), 16)
                            st_m[i] = sST.n
                        blk.sync(stq)
            blk.sync(lambda s: sST.wait(s))

    with nc.Block("rs_s1") as blk:
        def cc(g):
            sCC.inc(g.collective_compute(
                "ReduceScatter", ALU.add, replica_groups=RG,
                ins=[ctx.d["h_s1p"][:, :].opt()],
                outs=[ctx.d["h_s1"][:, :].opt()]), 1)
            g.wait_ge(sCC.h, sCC.n)
        blk.gpsimd(cc)


def phase_router(ctx):
    nc = ctx.nc
    sLD, sPE, sACT, sDV, sST = (ctx.S("ld"), ctx.S("pe"), ctx.S("act"),
                                ctx.S("dv"), ctx.S("st"))
    NTQ = (NCORE * TL) // TT
    cc_xg = ctx.d["cc_xg"]
    ident32 = ctx.masks["id32"]
    with contextlib.ExitStack() as st:
        routT = ctx.sb(st, "rt_routT", [128, CH * 16], BF16)
        selw = ctx.sb(st, "rt_selw", [16, 2], F32)
        xb = [ctx.sb(st, f"rt_xb{i}", [128, CH * TT], BF16) for i in range(2)]
        lg = ctx.sb(st, "rt_lg", [16, TT], F32)
        lgt = ctx.sb(st, "rt_lgt", [128, 4 * 16], F32)
        msk1 = ctx.sb(st, "rt_m1", [128, 16], F32)
        msk2 = ctx.sb(st, "rt_m2", [128, 16], F32)
        l2b = ctx.sb(st, "rt_l2", [128, 16], F32)
        tc = ctx.sb(st, "rt_tc", [128, 4], F32)
        sg = ctx.sb(st, "rt_sg", [128, 2], F32)
        w16t = ctx.sb(st, "rt_w16t", [16, TT], F32)
        w2t_s = ctx.sb(st, "rt_w2s", [2, TT], F32)
        plg = ctx.ps(st, "rt_plg", [16, TT], F32)
        ptr_ = [ctx.ps(st, f"rt_ptr{i}", [128, 16], F32) for i in range(2)]
        psig = ctx.ps(st, "rt_psig", [128, 2], F32)
        pw2 = ctx.ps(st, "rt_pw2", [2, TT], F32)
        pwt = ctx.ps(st, "rt_pwt", [16, 128], F32)

        with nc.Block("router") as blk:
            def init(g):
                sLD.inc(g.dma_start(
                    out=routT[:].rearrange("p (ci e) -> p ci e", ci=CH),
                    in_=ctx.d["routerT"].rearrange("(ci p) e -> p ci e", p=128)), 16)
            blk.gpsimd(init)

            def init2(s):
                sLD.inc(s.dma_start(out=selw[:], in_=ctx.d["sel2"][:, :]), 16)
            blk.sync(init2)

            rt_pe = {}
            last_wt = [0]
            w2st = {}
            for tq in range(NTQ):
                par = tq % 2

                def ldx(s, tq=tq, par=par):
                    if tq >= 2:
                        s.wait_ge(sPE.h, rt_pe[tq - 2])
                    sLD.inc(s.dma_start(
                        out=xb[par][:].rearrange("p (ci t) -> p ci t", ci=CH),
                        in_=cc_xg.rearrange("(r ci p) t -> p r ci t", r=NCORE, p=128)
                        [:, (tq * TT) // TL, :, (tq * TT) % TL:(tq * TT) % TL + TT]), 16)
                blk.sync(ldx)

                def pe_lg(t, tq=tq, par=par):
                    sLD.wait(t)
                    if tq > 0:
                        sACT.wait(t)
                    for ci in range(CH):
                        t.matmul(plg[:], routT[:, ci * 16:(ci + 1) * 16],
                                 xb[par][:, ci * TT:(ci + 1) * TT],
                                 start=(ci == 0), stop=(ci == CH - 1))
                    t.drain().then_inc(sPE.h, 1)
                    sPE.n += 1
                    rt_pe[tq] = sPE.n
                blk.tensor(pe_lg)

                def a_lg(a, tq=tq):
                    a.wait_ge(sPE.h, rt_pe[tq])
                    a.activation(out=lg[:], in_=plg[:], func=AF.Identity)
                    dinc(a, sACT)
                blk.scalar(a_lg)
                lg_mark = sACT.n

                dv_base = sDV.n

                def pe_tr(t, lg_mark=lg_mark, dv_base=dv_base):
                    t.wait_ge(sACT.h, lg_mark)
                    for sub in range(4):
                        if sub >= 2:
                            t.wait_ge(sDV.h, dv_base + sub - 1)
                        t.transpose(ptr_[sub % 2][:, 0:16],
                                    lg[:, sub * 128:(sub + 1) * 128],
                                    ident32[0:16, 0:16])
                        t.drain().then_inc(sPE.h, 1)
                        sPE.n += 1
                blk.tensor(pe_tr)
                tr_base = sPE.n - 4

                def dv_tr(v, tr_base=tr_base):
                    for sub in range(4):
                        v.wait_ge(sPE.h, tr_base + sub + 1)
                        sDV.inc(v.tensor_copy(
                            out=lgt[:, sub * 16:(sub + 1) * 16],
                            in_=ptr_[sub % 2][:, 0:16]), 1)
                    v.drain()
                blk.vector(dv_tr)

                for sub in range(4):
                    def dv_t2(v, sub=sub, lw=last_wt[0]):
                        v.wait_ge(sPE.h, lw)
                        L = lgt[:, sub * 16:(sub + 1) * 16]
                        v.reduce_max(out=tc[:, 0:1], in_=L, axis=AX.X)
                        v.drain()
                        v.tensor_scalar(out=msk1[:], in0=L,
                                        scalar1=tc[:, 0:1], scalar2=None,
                                        op0=ALU.is_equal)
                        v.drain()
                        v.scalar_tensor_tensor(out=l2b[:], in0=msk1[:],
                                               scalar=-1.0e30, in1=L,
                                               op0=ALU.mult, op1=ALU.add)
                        v.drain()
                        v.reduce_max(out=tc[:, 1:2], in_=l2b[:], axis=AX.X)
                        v.drain()
                        v.tensor_scalar(out=msk2[:], in0=l2b[:],
                                        scalar1=tc[:, 1:2], scalar2=None,
                                        op0=ALU.is_equal)
                        v.drain()
                        dinc(v, sDV)
                    blk.vector(dv_t2)
                    t2_mark = sDV.n

                    def a_sig(a, t2_mark=t2_mark):
                        a.wait_ge(sDV.h, t2_mark)
                        a.activation(out=psig[:], in_=tc[:, 0:2], func=AF.Sigmoid)
                        dinc(a, sACT)
                    blk.scalar(a_sig)
                    sig_mark = sACT.n

                    def dv_w(v, sig_mark=sig_mark):
                        v.wait_ge(sACT.h, sig_mark)
                        v.tensor_copy(out=sg[:], in_=psig[:])
                        v.drain()
                        v.tensor_scalar_mul(msk1[:], msk1[:], sg[:, 0:1])
                        v.drain()
                        v.scalar_tensor_tensor(out=l2b[:], in0=msk2[:],
                                               scalar=sg[:, 1:2], in1=msk1[:],
                                               op0=ALU.mult, op1=ALU.add)
                        v.drain()
                        dinc(v, sDV)
                    blk.vector(dv_w)
                    w_mark = sDV.n

                    def pe_wt(t, w_mark=w_mark):
                        t.wait_ge(sDV.h, w_mark)
                        t.transpose(pwt[:], l2b[:], ident32[:])
                        t.drain().then_inc(sPE.h, 1)
                        sPE.n += 1
                    blk.tensor(pe_wt)
                    wt_mark = sPE.n
                    last_wt[0] = wt_mark

                    def dv_wt(v, wt_mark=wt_mark, sub=sub):
                        v.wait_ge(sPE.h, wt_mark)
                        v.tensor_copy(out=w16t[:, sub * 128:(sub + 1) * 128],
                                      in_=pwt[:])
                        dinc(v, sDV)
                    blk.vector(dv_wt)

                wt_dv = sDV.n

                def pe_sel(t, wt_dv=wt_dv):
                    t.wait_ge(sDV.h, wt_dv)
                    t.matmul(pw2[:], selw[:], w16t[:], start=True, stop=True)
                    t.drain().then_inc(sPE.h, 1)
                    sPE.n += 1
                blk.tensor(pe_sel)
                sel_mark = sPE.n

                def dv_sel(v, tq=tq, sel_mark=sel_mark):
                    v.wait_ge(sPE.h, sel_mark)
                    if tq >= 1:
                        v.wait_ge(sST.h, w2st[tq - 1])
                    v.tensor_copy(out=w2t_s[:], in_=pw2[:])
                    dinc(v, sDV)
                blk.vector(dv_sel)
                sel_dv = sDV.n

                def st_w2(s, tq=tq, sel_dv=sel_dv):
                    s.wait_ge(sDV.h, sel_dv)
                    sST.inc(s.dma_start(
                        out=ctx.d["h_w2"][:, tq * TT:(tq + 1) * TT],
                        in_=w2t_s[:]), 16)
                    w2st[tq] = sST.n
                blk.sync(st_w2)
            blk.sync(lambda s: sST.wait(s))


def phase_ffn1(ctx):
    nc = ctx.nc
    sLD, sPE, sACT, sDV, sST, sCC = (ctx.S("ld"), ctx.S("pe"), ctx.S("act"),
                                     ctx.S("dv"), ctx.S("st"), ctx.S("cc"))
    NTQ = (NCORE * TL) // TT
    cc_xg = ctx.d["cc_xg"]
    with contextlib.ExitStack() as st:
        wg = ctx.sb(st, "f1_wg", [128, CH * 2048], BF16)
        wu = ctx.sb(st, "f1_wu", [128, CH * 2048], BF16)
        wd = ctx.sb(st, "f1_wd", [128, 16 * C], BF16)
        sgT = ctx.sb(st, "f1_sgT", [128, CH * 256], BF16)
        suT = ctx.sb(st, "f1_suT", [128, CH * 256], BF16)
        sdT = ctx.sb(st, "f1_sdT", [128, 2 * C], BF16)
        xb = [ctx.sb(st, f"f1_xb{i}", [128, CH * TT], BF16) for i in range(2)]
        w2t = [[ctx.sb(st, f"f1_w2t{i}_{row}", [1, TT], F32) for row in range(2)]
               for i in range(2)]
        hsil = [ctx.sb(st, f"f1_hs{i}", [128, TT], F32) for i in range(2)]
        h16 = ctx.sb(st, "f1_h16", [128, 18 * TT], BF16)
        w0s = ctx.sb(st, "f1_w0s", [128, TT], F32)
        w1s = ctx.sb(st, "f1_w1s", [128, TT], F32)
        ftmp = ctx.sb(st, "f1_ftmp", [128, TT], F32)
        fout = [ctx.sb(st, f"f1_fo{i}", [128, TT], F32) for i in range(2)]
        ones_row = ctx.sb(st, "f1_ones", [1, 128], F32)
        pg = [ctx.ps(st, f"f1_pg{i}", [128, TT], F32) for i in range(2)]
        pu = [ctx.ps(st, f"f1_pu{i}", [128, TT], F32) for i in range(2)]
        pa = ctx.ps(st, "f1_pa", [128, TT], F32)
        pb = ctx.ps(st, "f1_pb", [128, TT], F32)
        pc = ctx.ps(st, "f1_pc", [128, TT], F32)
        pw = ctx.ps(st, "f1_pw", [128, TT], F32)

        gu2 = ctx.d["gu2"]
        dn2 = ctx.d["dn2"]

        with nc.Block("ffn1") as blk:
            def init(g):
                for e in range(2):
                    sLD.inc(g.dma_start(
                        out=wg[:].rearrange("p (ci f) -> p ci f", ci=CH)
                        [:, :, e * 1024:(e + 1) * 1024],
                        in_=gu2.rearrange("e (ci p) f -> e p ci f", p=128)
                        [e][:, :, 0:1024]), 16)
                    sLD.inc(g.dma_start(
                        out=wu[:].rearrange("p (ci f) -> p ci f", ci=CH)
                        [:, :, e * 1024:(e + 1) * 1024],
                        in_=gu2.rearrange("e (ci p) f -> e p ci f", p=128)
                        [e][:, :, 1024:2048]), 16)
                    sLD.inc(g.dma_start(
                        out=wd[:].rearrange("p (sub c) -> p sub c", sub=16)
                        [:, e * 8:(e + 1) * 8, :],
                        in_=dn2.rearrange("e (sub p) c -> e p sub c", p=128)[e]), 16)
                sLD.inc(g.dma_start(
                    out=sgT[:].rearrange("p (ci u) -> p ci u", ci=CH),
                    in_=ctx.d["sgsT"].rearrange("(ci p) u -> p ci u", p=128)), 16)
                sLD.inc(g.dma_start(
                    out=suT[:].rearrange("p (ci u) -> p ci u", ci=CH),
                    in_=ctx.d["susT"].rearrange("(ci p) u -> p ci u", p=128)), 16)
                sLD.inc(g.dma_start(
                    out=sdT[:].rearrange("p (sub c) -> p sub c", sub=2),
                    in_=ctx.d["sdsT"].rearrange("(sub p) c -> p sub c", p=128)), 16)
            blk.gpsimd(init)

            def initv(v):
                v.memset(ones_row[:], 1.0)
                dinc(v, sDV)
            blk.vector(initv)

            xb_pe = {}
            for tq in range(NTQ):
                par = tq % 2

                def ldx(s, tq=tq, par=par):
                    if tq >= 2:
                        s.wait_ge(sPE.h, xb_pe[tq - 2])
                    sLD.inc(s.dma_start(
                        out=xb[par][:].rearrange("p (ci t) -> p ci t", ci=CH),
                        in_=cc_xg.rearrange("(r ci p) t -> p r ci t", r=NCORE, p=128)
                        [:, (tq * TT) // TL, :, (tq * TT) % TL:(tq * TT) % TL + TT]), 16)
                    for row in range(2):
                        sLD.inc(s.dma_start(
                            out=w2t[par][row][:],
                            in_=ctx.d["h_w2"][row:row + 1,
                                              tq * TT:(tq + 1) * TT]), 16)
                blk.sync(ldx)

                hm = {}
                for u in range(18):
                    def pe_gu(t, par=par, u=u, tq=tq):
                        if u == 0:
                            sLD.wait(t)
                            if tq == 0:
                                sDV.wait(t)
                        if u >= 2:
                            t.wait_ge(sDV.h, hm[u - 2])
                        for ci in range(CH):
                            if u < 16:
                                glhs = wg[:, ci * 2048 + u * 128:
                                          ci * 2048 + u * 128 + 128]
                                ulhs = wu[:, ci * 2048 + u * 128:
                                          ci * 2048 + u * 128 + 128]
                            else:
                                us = u - 16
                                glhs = sgT[:, ci * 256 + us * 128:
                                           ci * 256 + us * 128 + 128]
                                ulhs = suT[:, ci * 256 + us * 128:
                                           ci * 256 + us * 128 + 128]
                            t.matmul(pg[u % 2][:], glhs,
                                     xb[par][:, ci * TT:(ci + 1) * TT],
                                     start=(ci == 0), stop=(ci == CH - 1))
                            t.matmul(pu[u % 2][:], ulhs,
                                     xb[par][:, ci * TT:(ci + 1) * TT],
                                     start=(ci == 0), stop=(ci == CH - 1))
                        t.drain().then_inc(sPE.h, 1)
                        sPE.n += 1
                        hm[(u, "pe")] = sPE.n
                    blk.tensor(pe_gu)

                    def act_h(a, u=u):
                        a.wait_ge(sPE.h, hm[(u, "pe")])
                        a.activation(out=hsil[u % 2][:], in_=pg[u % 2][:],
                                     func=AF.Silu)
                        dinc(a, sACT)
                        hm[(u, "act")] = sACT.n
                    blk.scalar(act_h)

                    def dv_h(v, u=u):
                        v.wait_ge(sACT.h, hm[(u, "act")])
                        v.tensor_mul(h16[:, u * TT:(u + 1) * TT],
                                     hsil[u % 2][:], pu[u % 2][:])
                        dinc(v, sDV)
                        hm[u] = sDV.n
                    blk.vector(dv_h)

                def pe_wb(t):
                    t.wait_ge(sDV.h, hm[17])
                    t.matmul(pw[:], ones_row[:], w2t[par][0][:],
                             start=True, stop=True)
                    t.drain().then_inc(sPE.h, 1)
                    sPE.n += 1
                blk.tensor(pe_wb)
                wb1 = sPE.n

                def dv_wb(v, wb1=wb1):
                    v.wait_ge(sPE.h, wb1)
                    v.tensor_copy(out=w0s[:], in_=pw[:])
                    dinc(v, sDV)
                blk.vector(dv_wb)
                wb1d = sDV.n

                def pe_wb2(t, wb1d=wb1d, par=par):
                    t.wait_ge(sDV.h, wb1d)
                    t.matmul(pw[:], ones_row[:], w2t[par][1][:],
                             start=True, stop=True)
                    t.drain().then_inc(sPE.h, 1)
                    sPE.n += 1
                blk.tensor(pe_wb2)
                wb2 = sPE.n

                def dv_wb2(v, wb2=wb2):
                    v.wait_ge(sPE.h, wb2)
                    v.tensor_copy(out=w1s[:], in_=pw[:])
                    dinc(v, sDV)
                blk.vector(dv_wb2)
                wb2d = sDV.n

                dnm = {}
                for co in range(CH):
                    def pe_dn(t, co=co, wb2d=wb2d):
                        if co == 0:
                            t.wait_ge(sDV.h, wb2d)
                        else:
                            t.wait_ge(sDV.h, dnm[co - 1])
                        for sub in range(8):
                            t.matmul(pa[:],
                                     wd[:, sub * C + co * 128:
                                        sub * C + co * 128 + 128],
                                     h16[:, sub * TT:(sub + 1) * TT],
                                     start=(sub == 0), stop=(sub == 7))
                        for sub in range(8):
                            t.matmul(pb[:],
                                     wd[:, (8 + sub) * C + co * 128:
                                        (8 + sub) * C + co * 128 + 128],
                                     h16[:, (8 + sub) * TT:(9 + sub) * TT],
                                     start=(sub == 0), stop=(sub == 7))
                        for sub in range(2):
                            t.matmul(pc[:],
                                     sdT[:, sub * C + co * 128:
                                        sub * C + co * 128 + 128],
                                     h16[:, (16 + sub) * TT:(17 + sub) * TT],
                                     start=(sub == 0), stop=(sub == 1))
                        t.drain().then_inc(sPE.h, 1)
                        sPE.n += 1
                        dnm[(co, "pe")] = sPE.n
                    blk.tensor(pe_dn)

                    def dv_dn(v, co=co):
                        v.wait_ge(sPE.h, dnm[(co, "pe")])
                        if (co - 2, "st") in dnm:
                            v.wait_ge(sST.h, dnm[(co - 2, "st")])
                        fo = fout[co % 2]
                        v.tensor_mul(fo[:], pa[:], w0s[:])
                        v.drain()
                        v.tensor_mul(ftmp[:], pb[:], w1s[:])
                        v.drain()
                        v.tensor_add(fo[:], fo[:], ftmp[:])
                        v.drain()
                        v.tensor_add(fo[:], fo[:], pc[:])
                        dinc(v, sDV)
                        dnm[co] = sDV.n
                    blk.vector(dv_dn)

                    def stq(s, co=co, tq=tq):
                        s.wait_ge(sDV.h, dnm[co])
                        gt = tq * TT
                        rank = gt // TL
                        col = gt % TL
                        sST.inc(s.dma_start(
                            out=ctx.d["h_f1p"][rank * C + co * 128:
                                               rank * C + co * 128 + 128,
                                               col:col + TT],
                            in_=fout[co % 2][:]), 16)
                        dnm[(co, "st")] = sST.n
                    blk.sync(stq)
                xb_pe[tq] = sPE.n
            blk.sync(lambda s: sST.wait(s))

    with nc.Block("rs_f1") as blk:
        def cc(g):
            sCC.inc(g.collective_compute(
                "ReduceScatter", ALU.add, replica_groups=RG,
                ins=[ctx.d["h_f1p"][:, :].opt()],
                outs=[ctx.d["h_f1r"][:, :].opt()]), 1)
            g.wait_ge(sCC.h, sCC.n)
        blk.gpsimd(cc)


def phase_final(ctx):
    nc = ctx.nc
    sLD, sPE, sACT, sDV, sST = (ctx.S("ld"), ctx.S("pe"), ctx.S("act"),
                                ctx.S("dv"), ctx.S("st"))
    yout = ctx.d["yout"]
    ident32 = ctx.masks["id32"]
    with contextlib.ExitStack() as st:
        xb = ctx.sb(st, "fn_xb", [128, CH * TT], F32)
        s1b = ctx.sb(st, "fn_s1b", [128, CH * TT], F32)
        fb = ctx.sb(st, "fn_fb", [128, CH * TT], F32)
        gb = ctx.sb(st, "fn_gb", [128, CH * TT], F32)
        x2b = ctx.sb(st, "fn_x2b", [128, CH * TT], F32)
        otm = [ctx.sb(st, f"fn_otm{i}", [128, C], F32) for i in range(2)]
        ptr_ = [ctx.ps(st, f"fn_pt{i}", [128, 512], F32) for i in range(2)]

        with nc.Block("final") as blk:
            st_m = {}
            t_prev = [0]
            for tt in range(NT):
                def ld(s, tt=tt):
                    if tt >= 1:
                        sPE.wait(s)
                        sDV.wait(s)
                        sACT.wait(s)
                    sLD.inc(s.dma_start(
                        out=xb[:].rearrange("p (ch t) -> p ch t", ch=CH),
                        in_=fmv(ctx.d["h_x1"])[:, :, tt * TT:(tt + 1) * TT]), 16)
                    sLD.inc(s.dma_start(
                        out=s1b[:].rearrange("p (ch t) -> p ch t", ch=CH),
                        in_=fmv(ctx.d["h_s1"])[:, :, tt * TT:(tt + 1) * TT]), 16)
                    sLD.inc(s.dma_start(
                        out=fb[:].rearrange("p (ch t) -> p ch t", ch=CH),
                        in_=fmv(ctx.d["h_f1r"])[:, :, tt * TT:(tt + 1) * TT]), 16)
                blk.sync(ld)

                def act_g(a, tt=tt):
                    sLD.wait(a)
                    for ch in range(CH):
                        a.activation(out=gb[:, ch * TT:(ch + 1) * TT],
                                     in_=fb[:, ch * TT:(ch + 1) * TT],
                                     func=AF.Silu)
                    dinc(a, sACT)
                blk.scalar(act_g)

                def dv_x2(v, tt=tt):
                    sACT.wait(v)
                    for ch in range(CH):
                        v.tensor_mul(x2b[:, ch * TT:(ch + 1) * TT],
                                     s1b[:, ch * TT:(ch + 1) * TT],
                                     gb[:, ch * TT:(ch + 1) * TT])
                    v.drain()
                    for ch in range(CH):
                        v.tensor_add(x2b[:, ch * TT:(ch + 1) * TT],
                                     x2b[:, ch * TT:(ch + 1) * TT],
                                     xb[:, ch * TT:(ch + 1) * TT])
                    dinc(v, sDV)
                blk.vector(dv_x2)
                x2_mark = sDV.n

                for sub in range(4):
                    def pe_t(t, tt=tt, sub=sub, x2_mark=x2_mark, tp=t_prev[0]):
                        t.wait_ge(sDV.h, x2_mark)
                        if tp:
                            t.wait_ge(sDV.h, tp)
                        for ch in range(CH):
                            t.transpose(
                                ptr_[ch // 4][:, (ch % 4) * 128:(ch % 4) * 128 + 128],
                                x2b[:, ch * TT + sub * 128:
                                    ch * TT + sub * 128 + 128],
                                ident32[:])
                        t.drain().then_inc(sPE.h, 1)
                        sPE.n += 1
                    blk.tensor(pe_t)
                    pe_mark = sPE.n

                    def dv_t(v, tt=tt, sub=sub, pe_mark=pe_mark):
                        v.wait_ge(sPE.h, pe_mark)
                        i = tt * 4 + sub
                        if i >= 2:
                            v.wait_ge(sST.h, st_m[i - 2])
                        v.tensor_copy(out=otm[sub % 2][:, 0:512],
                                      in_=ptr_[0][:])
                        v.tensor_copy(out=otm[sub % 2][:, 512:1024],
                                      in_=ptr_[1][:])
                        dinc(v, sDV)
                    blk.vector(dv_t)
                    t_mark = sDV.n
                    t_prev[0] = t_mark

                    def stq(s, tt=tt, sub=sub, t_mark=t_mark):
                        s.wait_ge(sDV.h, t_mark)
                        r0 = tt * TT + sub * 128
                        sST.inc(s.dma_start(out=yout[r0:r0 + 128, :],
                                            in_=otm[sub % 2][:]), 16)
                        st_m[tt * 4 + sub] = sST.n
                    blk.sync(stq)
            blk.sync(lambda s: sST.wait(s))


def _cols(a):
    return np.ascontiguousarray(np.asarray(a, np.float32).reshape(CH, 128).T)


def make_core_inputs_l0(inputs, core):
    b, q = core // 4, core % 4
    t0 = q * TL
    x = inputs["L0_attn_x"][b]
    xin = np.zeros((PAD + TL, C), np.float32)
    lo = max(0, t0 - PAD)
    xin[PAD - (t0 - lo):PAD] = x[lo:t0]
    xin[PAD:] = x[t0:t0 + TL]
    prev = np.full((128, 1), 1.0 if q > 0 else 0.0, np.float32)
    pw = np.zeros((1, 8), np.float32)
    msk = np.zeros((1, 8), np.float32)
    for j in range(8):
        jb, jq = j // 4, j % 4
        if jb == b and jq < q:
            msk[0, j] = 1.0
            pw[0, j] = float((q - 1 - jq) * TL)
    return {
        "xin": np.ascontiguousarray(xin),
        "prev_scale": prev,
        "pw": np.ascontiguousarray(np.repeat(pw, 128, axis=0)),
        "msk": np.ascontiguousarray(np.tile(np.repeat(msk, 128, axis=0), (1, 8))),
        "ln0_g": _cols(inputs["L0_attn_ln_g"]),
        "ln0_b": _cols(inputs["L0_attn_ln_b"]),
        "tmk": _cols(inputs["L0_attn_time_mix_k"]),
        "tmv": _cols(inputs["L0_attn_time_mix_v"]),
        "tmr": _cols(inputs["L0_attn_time_mix_r"]),
        "time_decay": _cols(inputs["L0_attn_time_decay"]),
        "Wk": np.ascontiguousarray(inputs["L0_attn_Wk"], np.float32),
        "Wv": np.ascontiguousarray(inputs["L0_attn_Wv"], np.float32),
        "Wr": np.ascontiguousarray(inputs["L0_attn_Wr"], np.float32),
        "Wo": np.ascontiguousarray(inputs["L0_attn_Wo"], np.float32),
        "wg_e": np.ascontiguousarray(inputs["L0_ffn_Wg"][core], np.float32),
        "wu_e": np.ascontiguousarray(inputs["L0_ffn_Wu"][core], np.float32),
        "wd_e": np.ascontiguousarray(inputs["L0_ffn_Wd"][core], np.float32),
        "wqkv_s": np.ascontiguousarray(
            inputs["L1_attn_W_qkv"].reshape(6, 8, 64, C)[:, core, :, :]
            .reshape(384, C), np.float32),
        "wout_sT": np.ascontiguousarray(
            inputs["L1_attn_W_out"][:, core * 64:(core + 1) * 64].T, np.float32),
        "routerT": np.ascontiguousarray(inputs["L1_ffn_router_w"].T, np.float32),
        "sel2": _sel2(core),
        "gu2": np.ascontiguousarray(
            inputs["L1_ffn_gate_up_w"][2 * core:2 * core + 2], np.float32),
        "dn2": np.ascontiguousarray(
            inputs["L1_ffn_down_w"][2 * core:2 * core + 2], np.float32),
        "sgsT": np.ascontiguousarray(
            inputs["L1_ffn_shared_gate_w"][core * 256:(core + 1) * 256].T,
            np.float32),
        "susT": np.ascontiguousarray(
            inputs["L1_ffn_shared_up_w"][core * 256:(core + 1) * 256].T,
            np.float32),
        "sdsT": np.ascontiguousarray(
            inputs["L1_ffn_shared_down_w"][:, core * 256:(core + 1) * 256].T,
            np.float32),
    }


def _sel2(core):
    s = np.zeros((16, 2), np.float32)
    s[2 * core, 0] = 1.0
    s[2 * core + 1, 1] = 1.0
    return s


def declare_l0_inputs(ctx):
    ctx.dram("xin", [PAD + TL, C], F32, kind="ExternalInput")
    ctx.dram("prev_scale", [128, 1], F32, kind="ExternalInput")
    ctx.dram("pw", [128, 8], F32, kind="ExternalInput")
    ctx.dram("msk", [128, 64], F32, kind="ExternalInput")
    for nm in ("ln0_g", "ln0_b", "tmk", "tmv", "tmr", "time_decay"):
        ctx.dram(nm, [128, CH], F32, kind="ExternalInput")
    for w in ("Wk", "Wv", "Wr", "Wo"):
        ctx.dram(w, [C, C], F32, kind="ExternalInput")
    ctx.dram("wg_e", [C, 4096], F32, kind="ExternalInput")
    ctx.dram("wu_e", [C, 4096], F32, kind="ExternalInput")
    ctx.dram("wd_e", [4096, C], F32, kind="ExternalInput")


def declare_l0_scratch(ctx, dbg=()):
    def kind(n):
        return "ExternalOutput" if n in dbg else "Internal"
    ctx.dram("h_x", [C, PAD + TL], F32, kind=kind("h_x"))
    ctx.dram("h_xn16", [C, TL], BF16, kind=kind("h_xn16"))
    ctx.dram("h_xnf", [C, PAD + TL], F32, kind=kind("h_xnf"))
    ctx.dram("h_rows", [10, 6 * TT], F32, kind=kind("h_rows"))
    ctx.dram("h_xn2f", [C, PAD + TL], F32, kind=kind("h_xn2f"))
    ctx.dram("h_ek", [C, TL], F32, kind=kind("h_ek"))
    ctx.dram("h_kv", [C, TL], F32, kind=kind("h_kv"))
    ctx.dram("h_num", [C, TL], F32, kind=kind("h_num"))
    ctx.dram("h_den", [C, TL], F32, kind=kind("h_den"))
    ctx.dram("h_sigr", [C, TL], BF16, kind=kind("h_sigr"))
    ctx.dram("h_scols", [128, 16], F32, kind=kind("h_scols"))
    ctx.dram("h_fg", [128, 128], F32, kind=kind("h_fg"))
    ctx.dram("h_fgt", [128, 128], F32, kind=kind("h_fgt"))
    ctx.dram("h_pb8", [128, 16], F32, kind=kind("h_pb8"))
    ctx.dram("h_coef0", [128, 8], F32, kind=kind("h_coef0"))
    ctx.dram("h_rsdbg", [128, 32], F32, kind=kind("h_rsdbg"))
    for w in ("h_wkT", "h_wvT", "h_wrT", "h_woT"):
        ctx.dram(w, [C, C], BF16, kind=kind(w))
    ctx.dram("cc_fin", [16, 128], F32, kind="Internal")
    ctx.dram("cc_fout", [128, 128], F32, kind="Internal")


def build_launch_A(dbg=()):
    nc = bass.Bass("TRN2", target_bir_lowering=False, debug=False)
    with contextlib.ExitStack() as stack:
        ctx = Ctx(nc, stack)
        declare_l0_inputs(ctx)
        declare_l0_scratch(ctx, dbg)
        def kind(n):
            return "ExternalOutput" if n in dbg else "Internal"
        ctx.dram("h_s0", [C, TL], F32, kind=kind("h_s0"))
        ctx.dram("h_g0", [C, TL], F32, kind=kind("h_g0"))
        ctx.dram("h_x1", [C, TL], F32, kind=kind("h_x1"))
        ctx.dram("h_xn1", [C, TL], BF16, kind="Internal")
        if "h_xn1" in dbg:
            ctx.dram("h_xn1_dbg", [C, TL], BF16, kind="ExternalOutput")
        ctx.dram("cc_xg", [NCORE * C, TL], BF16, kind="Internal")
        ctx.dram("cc_x0", [NCORE * C, TL], BF16, kind="Internal")
        ctx.dram("h_f0p", [NCORE * C, TL], F32, kind="Internal")
        ctx.dram("h_f0r", [C, TL], F32, kind="Internal")
        ctx.dram("wqkv_s", [384, C], F32, kind="ExternalInput")
        ctx.dram("wout_sT", [64, C], F32, kind="ExternalInput")
        ctx.dram("routerT", [C, 16], F32, kind="ExternalInput")
        ctx.dram("sel2", [16, 2], F32, kind="ExternalInput")
        ctx.dram("gu2", [2, C, 2048], F32, kind="ExternalInput")
        ctx.dram("dn2", [2, C, C], F32, kind="ExternalInput")
        ctx.dram("sgsT", [C, 256], F32, kind="ExternalInput")
        ctx.dram("susT", [C, 256], F32, kind="ExternalInput")
        ctx.dram("sdsT", [256, C], F32, kind="ExternalInput")
        ctx.dram("h_wqkvT", [C, 384], BF16, kind="Internal")
        ctx.dram("h_lhc", [2 * 8 * 64 * 128, 1024], BF16, kind="Internal")
        ctx.dram("h_out", [128, T], BF16, kind="Internal")
        if "h_out" in dbg:
            ctx.dram("h_out_dbg", [128, T], BF16, kind="ExternalOutput")
        import os as _os
        if int(_os.environ.get("ATT_DBGS", "-1")) >= 0:
            _S = int(_os.environ.get("ATT_DBGS"))
            ctx.dram("h_t1_dbg", [128, (4 * _S + 4) * 512], BF16,
                     kind="ExternalOutput")
            ctx.dram("h_lh_dbg", [16 * 128, 1024], BF16, kind="ExternalOutput")
        ctx.dram("h_s1p", [NCORE * C, TL], F32, kind="Internal")
        ctx.dram("h_s1", [C, TL], F32, kind="Internal")
        ctx.dram("h_w2", [2, NCORE * TL], F32, kind=kind("h_w2"))
        ctx.dram("h_f1p", [NCORE * C, TL], F32, kind="Internal")
        ctx.dram("h_f1r", [C, TL], F32, kind="Internal")
        ctx.dram("yout", [TL, C], F32, kind="ExternalOutput")
        for nm in ("ld", "pe", "act", "dv", "po", "st", "cc"):
            ctx.S(nm)
        with nc.Block("clr") as b0:
            def clr(g):
                for sem in ctx.sems.values():
                    g.sem_clear(sem.h)
            b0.gpsimd(clr)
        import os
        skip = os.environ.get("SKIP_PHASES", "")
        phase_prep_masks(ctx)
        if "w0" not in skip:
            phase_prep_w0(ctx)
        if "ln" not in skip:
            phase_ln0(ctx)
        if "ra" not in skip:
            phase_rwkv_a(ctx)
        if "rs" not in skip:
            phase_rwkv_s(ctx)
        if "rb" not in skip:
            phase_rwkv_b(ctx)
        if "f0" not in skip:
            phase_ag_xn0(ctx)
            phase_ffn0(ctx)
            phase_rs_f0(ctx)
        if "r1" not in skip:
            phase_resid_ln1(ctx)
        if "ag1" not in skip:
            phase_ag_xn1(ctx)
        if "w1" not in skip:
            phase_prep_w1(ctx)
        if "at" not in skip:
            phase_l1_attn(ctx, 0)
            phase_l1_attn(ctx, 1)
        if "s1" not in skip:
            phase_s1p(ctx)
        if "rt" not in skip:
            phase_router(ctx)
        if "f1" not in skip:
            phase_ffn1(ctx)
        for nm in ("h_s1", "h_f1r"):
            if nm in dbg:
                ctx.dram(nm + "_dbg", [C, TL], F32, kind="ExternalOutput")
                with nc.Block(f"dbg_{nm}") as blk:
                    def cp(s, nm=nm):
                        sST = ctx.S("st")
                        sST.inc(s.dma_start(out=ctx.d[nm + "_dbg"][:, :],
                                            in_=ctx.d[nm][:, :]), 16)
                        sST.wait(s)
                    blk.sync(cp)
        if "fin" not in skip:
            phase_final(ctx)
    return nc


def kernel(**inputs):
    nc = build_launch_A()
    in_maps = [make_core_inputs_l0(inputs, c) for c in range(NCORE)]
    res = run_bass_kernel_spmd(nc, in_maps, core_ids=list(range(NCORE)))
    out = np.zeros((B, T, C), np.float32)
    for c in range(NCORE):
        b, q = c // 4, c % 4
        out[b, q * TL:(q + 1) * TL, :] = res.results[c]["yout"]
    return out
